# version 06
# baseline (speedup 1.0000x reference)
import numpy as np
import ml_dtypes
from contextlib import ExitStack

import concourse.bass as bass
import concourse.mybir as mybir
from concourse.bass_utils import run_bass_kernel_spmd

F32 = mybir.dt.float32
BF16 = mybir.dt.bfloat16
I32 = mybir.dt.int32
AF = mybir.ActivationFunctionType
OP = mybir.AluOpType
AXX = mybir.AxisListType.X

D = 2048
T = 1024
TL = 4096
DL = 1024
NCD = 16
NDL = 8
CH = 128 * 1024
B_KQ = 16 * CH
B_V = 8 * CH
B_A = 24 * CH
B_HO = 2 * 128 * TL
GROUPS = [[0, 1, 2, 3], [4, 5, 6, 7]]


def APx(t, off, dims):
    return bass.AP(t, off, [list(d) for d in dims])


class Prog:
    ENGINES = ("tensor", "vector", "scalar", "sync", "gpsimd")

    def __init__(self):
        self.ops = {e: [] for e in self.ENGINES}
        self.cnt = {}

    def add(self, eng, fn, wait=()):
        self.ops[eng].append(([(s, v) for (s, v) in wait], fn))

    def bump(self, sem, n=1):
        self.cnt[sem.name] = self.cnt.get(sem.name, 0) + n
        return self.cnt[sem.name]

    def realize(self, block):
        def runner(ename):
            def run(eng):
                for waits, fn in self.ops[ename]:
                    for s, v in waits:
                        eng.wait_ge(s, v)
                    fn(eng)
            return run
        block.tensor(runner("tensor"))
        block.vector(runner("vector"))
        block.scalar(runner("scalar"))
        block.sync(runner("sync"))
        block.gpsimd(runner("gpsimd"))


def build_nc(stage=99):
    nc = bass.Bass("TRN2", target_bir_lowering=False)
    es = ExitStack()

    din = {}
    for name, shape in [("xt", [D, T]), ("wk", [D, D]), ("wq", [D, D]), ("wv", [D, D]),
                        ("wkl", [D, DL]), ("wql", [D, DL]), ("wvl", [D, DL]),
                        ("wo", [DL, D]), ("fwt", [D, D]), ("frt", [D, D])]:
        din[name] = nc.dram_tensor(name, shape, F32, kind="ExternalInput")
    for name, n in [("bk", D), ("bq", D), ("bv", D), ("bo", D),
                    ("bkl", DL), ("bql", DL), ("bvl", DL)]:
        din[name] = nc.dram_tensor(name, [n], F32, kind="ExternalInput")
    din["tbl"] = nc.dram_tensor("tbl", [1, 8], I32, kind="ExternalInput")
    out_t = nc.dram_tensor("out_t", [D, T], F32, kind="ExternalOutput")
    dbg = {}
    if stage == 13:
        for nm, sh, dt in [("d_xh", [128, NCD * T], BF16), ("d_c1", [128, NCD * T], BF16),
                           ("d_bvl", [128, 1024], F32), ("d_ccv2", [1, B_V], BF16)]:
            dbg[nm] = nc.dram_tensor(nm, sh, dt, kind="ExternalOutput")
    if stage == 20:
        for nm, sh in [("d_k0", [128, TL]), ("d_q0", [128, TL]), ("d_va0", [128, 4160]),
                       ("d_ho", [128, 8192]), ("d_bias", [128, 8]), ("d_hoall", [128, 8192])]:
            dbg[nm] = nc.dram_tensor(nm, sh, F32, kind="ExternalOutput")
        dbg["d_ccv"] = nc.dram_tensor("d_ccv", [1, B_V], BF16, kind="ExternalOutput")
        dbg["d_scrow"] = nc.dram_tensor("d_scrow", [4, 16], F32, kind="ExternalOutput")
        dbg["d_kq2"] = nc.dram_tensor("d_kq2", [128, 4096], F32, kind="ExternalOutput")

    PAD = 4 * 1024 * 1024
    wr2_hbm = nc.dram_tensor("wr2_hbm", [NCD * 128 * D], BF16)
    nc.dram_tensor("pad0", [1, PAD], BF16)
    ccain = nc.dram_tensor("ccain", [1, B_A], BF16)
    nc.dram_tensor("pad1", [1, PAD], BF16)
    ccaout = nc.dram_tensor("ccaout", [1, 4 * B_A], BF16)
    nc.dram_tensor("pad2", [1, PAD], BF16)
    cc2_in = nc.dram_tensor("cc2_in", [1, B_HO], BF16)
    nc.dram_tensor("pad5", [1, PAD], BF16)
    cc2_out = nc.dram_tensor("cc2_out", [1, 4 * B_HO], BF16)
    nc.dram_tensor("pad6", [1, PAD], BF16)

    sQ = es.enter_context(nc.semaphore("sQ"))
    sG = es.enter_context(nc.semaphore("sG"))
    sCC = es.enter_context(nc.semaphore("sCC"))
    sPE = es.enter_context(nc.semaphore("sPE"))
    sV = es.enter_context(nc.semaphore("sV"))
    sA = es.enter_context(nc.semaphore("sA"))
    rg0 = es.enter_context(nc.gpsimd.register("rg0"))
    rg1 = es.enter_context(nc.gpsimd.register("rg1"))

    def sbt(name, shape, dt):
        return es.enter_context(nc.sbuf_tensor(name, shape, dt))
    x_res = sbt("x_res", [128, NCD * T], F32)
    ones_f = sbt("ones_f", [128, 128], F32)
    ones_bf = sbt("ones_bf", [128, 1], BF16)
    tri01 = sbt("tri01", [128, 128], BF16)
    mk4 = sbt("mk4", [128, 2048], BF16)
    tmpM = sbt("tmpM", [128, 128], F32)
    m_row = sbt("m_row", [1, 1024], F32)
    aux_row = sbt("aux_row", [1, 1024], F32)
    thrp_row = sbt("thrp_row", [1, 1024], F32)
    thrm_row = sbt("thrm_row", [1, 1024], F32)
    sc_row = sbt("sc_row", [1, 16], F32)
    bias4 = sbt("bias4", [128, 8], F32)
    bk_sb = sbt("bk_sb", [128, 16], F32)
    bq_sb = sbt("bq_sb", [128, 16], F32)
    bv_sb = sbt("bv_sb", [128, 16], F32)
    bo_sb = sbt("bo_sb", [128, 16], F32)
    bkl_sb = sbt("bkl_sb", [128, 8], F32)
    bql_sb = sbt("bql_sb", [128, 8], F32)
    tbl_sb = sbt("tbl_sb", [1, 8], I32)
    gq_tmp = sbt("gq_tmp", [128, 512], BF16)
    lq_tmp = sbt("lq_tmp", [128, 512], BF16)
    scrB = sbt("scrB", [128, 16384], BF16)
    ones_b2 = sbt("ones_b2", [128, 128], BF16)
    negr = sbt("negr", [1, 128], BF16)
    bcB = sbt("bcB", [128, 1024], F32)
    act_bf = sbt("act_bf", [128, NCD * T], BF16)

    psAB = es.enter_context(nc.psum_tensor("psAB", [128, 2048], F32))
    psC = es.enter_context(nc.psum_tensor("psC", [128, 1024], F32))
    psD = es.enter_context(nc.psum_tensor("psD", [128, 1024], F32))

    class PsView:
        def __init__(self, t, off):
            self.t, self.off = t, off

        def __getitem__(self, key):
            sp, sc = key
            return self.t[sp, self.off + sc.start: self.off + sc.stop]
    psA = PsView(psAB, 0)
    psB = PsView(psAB, 1024)
    psS = [psA, psB]
    psALL = [psA, psB, psC, psD]

    P = Prog()
    bump = P.bump
    aux_stacks = []

    xbf2 = lambda s: scrB[:, 1024 * s: 1024 * (s + 1)]
    xsq2 = lambda s: scrB[:, 2048 + 1024 * s: 3072 + 1024 * s]
    wblk = lambda s: scrB[:, 4096 + 2048 * s: 6144 + 2048 * s]
    wblk4 = lambda s: scrB[:, 4096 + 2048 * (s % 4): 6144 + 2048 * (s % 4)]
    wvlt = lambda s: scrB[:, 8192 + 1024 * s: 9216 + 1024 * s]
    kslot = lambda s: scrB[:, 10240 + 1024 * s: 11264 + 1024 * s]
    vslot = lambda s: scrB[:, 12288 + 1024 * s: 13312 + 1024 * s]
    pt_off = lambda s: 1024 * s
    kq2v = scrB[:, 2048: 6144]
    ho_off = lambda p: 6144 + 4096 * p
    wot = lambda s: scrB[:, 14336 + 1024 * s: 14336 + 1024 * (s + 1)]

    x_c = lambda dc: x_res[:, T * dc: T * (dc + 1)]
    x_ch = lambda dc, th: x_res[:, T * dc + 512 * th: T * dc + 512 * (th + 1)]
    actc = lambda ic, th: act_bf[:, T * ic + 512 * th: T * ic + 512 * th + 512]

    def f(sy):
        sy.dma_start(out=tbl_sb[:, :], in_=din["tbl"].ap()).then_inc(sQ, 16)
        for name, dst, n in [("bk", bk_sb, 16), ("bq", bq_sb, 16), ("bv", bv_sb, 16),
                             ("bo", bo_sb, 16), ("bkl", bkl_sb, 8), ("bql", bql_sb, 8)]:
            sy.dma_start(out=dst[:, 0:n], in_=APx(din[name], 0, [[1, 128], [128, n]]),
                         allow_slow_non_contiguous=True).then_inc(sQ, 16)
        for i in range(4):
            sy.dma_start(
                out=APx(x_res, 4 * T * i, [[NCD * T, 128], [T, 4], [1, T]]),
                in_=APx(din["xt"], 4 * T * 128 * i, [[T, 128], [128 * T, 4], [1, T]]),
            ).then_inc(sQ, 16)
    P.add("sync", f)
    bump(sQ, 16 * 11)
    x_done = [16 * 7 + 16 * (i + 1) for i in range(4)]

    def f(gp):
        gp.memset(ones_f[:, :], 1.0)
        gp.memset(ones_b2[:, :], 1.0)
        gp.memset(negr[:, :], -0.0625)
        gp.memset(ones_bf[:, :], 1.0)
        gp.memset(bias4[:, 0:1], -0.2)
        gp.memset(bias4[:, 1:2], 0.2)
        gp.memset(bias4[:, 2:3], 1e-5)
        gp.memset(bias4[:, 5:6], 0.0)
        gp.iota(tmpM[:, :], [[-1, 128]], base=0, channel_multiplier=1,
                allow_small_or_imprecise_dtypes=True).then_inc(sG, 1)
    P.add("gpsimd", f)
    g_init = bump(sG, 1)

    def f(gp):
        for i in range(4):
            gp.iota(bcB[:, 0:512], [[-1, 512]], base=128 * i, channel_multiplier=1,
                    allow_small_or_imprecise_dtypes=True).then_inc(sG, 1)
            gp.wait_ge(sV, 1 + i)
        gp.nop()
    P.add("gpsimd", f, wait=[(sG, g_init)])

    for i in range(4):
        def f(v, i=i):
            v.tensor_single_scalar(mk4[:, 512 * i:512 * (i + 1)], bcB[:, 0:512], 0.0,
                                   op=OP.is_le).then_inc(sV, 1)
        P.add("vector", f, wait=[(sG, g_init + 1 + i)])
        bump(sV, 1)
    bump(sG, 4)

    def emit_ln_stats(ready, mode):
        mm_cnt = []
        for dc in range(NCD):
            s = dc % 2
            w = list(ready(dc))
            if dc >= 2:
                w.append((sPE, mm_cnt[dc - 2]))
            def f(sc, dc=dc, s=s):
                sc.activation(xbf2(s)[:, :], x_c(dc), AF.Copy)
                sc.activation(xsq2(s)[:, :], x_c(dc), AF.Square).then_inc(sA, 1)
            P.add("scalar", f, wait=w)
            a_done = bump(sA, 1)

            def f(t, dc=dc, s=s):
                st, sp = dc == 0, dc == NCD - 1
                t.matmul(psC[0:1, 0:512], ones_bf[:, 0:1], xbf2(s)[:, 0:512], start=st, stop=sp)
                t.matmul(psC[0:1, 512:1024], ones_bf[:, 0:1], xbf2(s)[:, 512:1024], start=st, stop=sp)
                t.matmul(psD[0:1, 0:512], ones_bf[:, 0:1], xsq2(s)[:, 0:512], start=st, stop=sp)
                t.matmul(psD[0:1, 512:1024], ones_bf[:, 0:1], xsq2(s)[:, 512:1024],
                         start=st, stop=sp).then_inc(sPE, 1)
            P.add("tensor", f, wait=[(sA, a_done)])
            mm_cnt.append(bump(sPE, 1))

        def f(v):
            v.tensor_scalar_mul(m_row[:, :], psC[0:1, :], 1.0 / D)
            v.tensor_scalar_mul(aux_row[:, :], psD[0:1, :], 1.0 / D)
            v.tensor_tensor(thrp_row[:, :], m_row[:, :], m_row[:, :], op=OP.mult)
            v.tensor_tensor(aux_row[:, :], aux_row[:, :], thrp_row[:, :],
                            op=OP.subtract).then_inc(sV, 1)
        P.add("vector", f, wait=[(sPE, mm_cnt[-1])])
        v1 = bump(sV, 1)

        def f(sc):
            sc.activation(aux_row[:, :], aux_row[:, :], AF.Sqrt,
                          bias=bias4[0:1, 2:3]).then_inc(sA, 1)
        P.add("scalar", f, wait=[(sV, v1)])
        a1 = bump(sA, 1)

        if mode == "thr":
            def f(v):
                v.scalar_tensor_tensor(thrp_row[:, :], aux_row[:, :], 0.33, m_row[:, :],
                                       op0=OP.mult, op1=OP.add)
                v.scalar_tensor_tensor(thrm_row[:, :], aux_row[:, :], -0.33, m_row[:, :],
                                       op0=OP.mult, op1=OP.add).then_inc(sV, 1)
            rows = (thrp_row, thrm_row)
        else:
            def f(v):
                v.reciprocal(aux_row[:, :], aux_row[:, :]).then_inc(sV, 1)
            rows = (m_row, aux_row)
        P.add("vector", f, wait=[(sA, a1)])
        v2 = bump(sV, 1)

        h0 = scrB[0:1, 10240:11264]
        l0 = scrB[0:1, 11264:12288]
        h1 = scrB[0:1, 12288:13312]
        l1 = scrB[0:1, 13312:14336]

        def f(v, rows=rows):
            v.tensor_copy(h0, rows[0][0:1, :])
            v.tensor_tensor(l0, rows[0][0:1, :], h0, op=OP.subtract)
            v.tensor_copy(h1, rows[1][0:1, :])
            v.tensor_tensor(l1, rows[1][0:1, :], h1, op=OP.subtract).then_inc(sV, 1)
        P.add("vector", f, wait=[(sV, v2)])
        v3 = bump(sV, 1)

        def f(t):
            ob = ones_b2[0:1, 0:128]
            last = None
            for th in range(2):
                sl = slice(512 * th, 512 * (th + 1))
                t.matmul(psC[:, sl], ob, h0[0:1, sl], start=True, stop=False)
                t.matmul(psC[:, sl], ob, l0[0:1, sl], start=False, stop=True)
                t.matmul(psD[:, sl], ob, h1[0:1, sl], start=True, stop=False)
                last = t.matmul(psD[:, sl], ob, l1[0:1, sl], start=False, stop=True)
            last.then_inc(sPE, 1)
        P.add("tensor", f, wait=[(sV, v3)])
        return (sPE, bump(sPE, 1))

    def emit_quantize(bc_w):
        half = None
        for dc in range(NCD):
            for th in range(2):
                def f(v, dc=dc, th=th):
                    sl = slice(512 * th, 512 * (th + 1))
                    v.tensor_tensor(gq_tmp[:, :], x_ch(dc, th), psC[:, sl], op=OP.is_gt)
                    v.tensor_tensor(lq_tmp[:, :], x_ch(dc, th), psD[:, sl], op=OP.is_lt)
                    v.tensor_tensor(actc(dc, th), gq_tmp[:, :], lq_tmp[:, :],
                                    op=OP.subtract).then_inc(sV, 1)
                P.add("vector", f, wait=[bc_w])
                bump(sV, 1)
            if dc == 7:
                half = P.cnt[sV.name]
        return P.cnt[sV.name], half

    es0 = ExitStack()
    aux_stacks.append(es0)
    fwtst = [es0.enter_context(nc.sbuf_tensor(f"fwtst{i}", [128, 2048], F32)) for i in range(2)]
    frtst = [es0.enter_context(nc.sbuf_tensor(f"frtst{i}", [128, 2048], F32)) for i in range(2)]
    s1t = [es0.enter_context(nc.sbuf_tensor(f"s1t{i}", [128, 2048], BF16)) for i in range(2)]
    s2t = [es0.enter_context(nc.sbuf_tensor(f"s2t{i}", [128, 2048], BF16)) for i in range(2)]

    def finish(wait):
        def f(sy):
            for i in range(4):
                sy.dma_start(
                    out=APx(out_t, 4 * T * 128 * i, [[T, 128], [128 * T, 4], [1, T]]),
                    in_=APx(x_res, 4 * T * i, [[NCD * T, 128], [T, 4], [1, T]]),
                ).then_inc(sQ, 16)
        P.add("sync", f, wait=wait)
        od = bump(sQ, 64)
        def f(sy):
            pass
        P.add("sync", f, wait=[(sQ, od)])
        with nc.Block() as block:
            P.realize(block)
        for st in reversed(aux_stacks):
            try:
                st.close()
            except Exception:
                pass
        es.close()
        return nc

    if stage == 9:
        return finish([(sQ, x_done[-1])])

    bc1 = emit_ln_stats(lambda dc: [(sQ, x_done[dc // 4])], "thr")
    if stage == 10:
        return finish([bc1])
    qz1, qz_half = emit_quantize(bc1)
    if stage == 11:
        return finish([(sV, qz1)])

    prep_q, prep_g, prep_a, prep_v = [], [], [], []
    wr2_st = []
    ffn1_mm, ffn1_evac = [], []
    for og in range(NCD):
        s = og % 2
        wl = [] if og < 2 else [(sV, prep_v[og - 2])]
        def f(sy, og=og, s=s):
            sy.dma_start(out=fwtst[s][:, :],
                         in_=APx(din["fwt"], og * 128, [[D, 128], [128 * D, 16], [1, 128]])).then_inc(sQ, 16)
        P.add("sync", f, wait=wl)
        prep_q.append(bump(sQ, 16))
        def f(gp, og=og, s=s):
            gp.dma_start(out=frtst[s][:, :],
                         in_=APx(din["frt"], og * 128, [[D, 128], [128 * D, 16], [1, 128]])).then_inc(sG, 16)
        P.add("gpsimd", f, wait=wl)
        prep_g.append(bump(sG, 16))

        wl = [(sQ, prep_q[og]), (sG, prep_g[og])]
        if og >= 2:
            wl.append((sV, prep_v[og - 2]))
        def f(sc, s=s):
            sc.activation(s1t[s][:, :], fwtst[s][:, :], AF.Sign, bias=bias4[:, 0:1])
            sc.activation(s2t[s][:, :], fwtst[s][:, :], AF.Sign, bias=bias4[:, 1:2]).then_inc(sA, 1)
        P.add("scalar", f, wait=wl)
        prep_a.append(bump(sA, 1))

        wl = [(sA, prep_a[og])]
        if og >= 4:
            wl.append((sPE, ffn1_mm[og - 4]))
            wl.append((sQ, wr2_st[og - 4]))
        def f(v, og=og, s=s):
            v.tensor_tensor(s1t[s][:, :], s1t[s][:, :], s2t[s][:, :], op=OP.add)
            v.scalar_tensor_tensor(wblk4(og)[:, :], s1t[s][:, :], 0.3, frtst[s][:, :],
                                   op0=OP.mult, op1=OP.add).then_inc(sV, 1)
        P.add("vector", f, wait=wl)
        prep_v.append(bump(sV, 1))

        def f(sy, og=og):
            sy.dma_start(out=APx(wr2_hbm, og * 128 * D, [[D, 128], [1, D]]),
                         in_=wblk4(og)[:, :]).then_inc(sQ, 16)
        P.add("sync", f, wait=[(sV, prep_v[og])])
        wr2_st.append(bump(sQ, 16))

        if stage == 12:
            ffn1_mm.append(P.cnt.get(sPE.name, 0))
            ffn1_evac.append(P.cnt.get(sV.name, 0))
            continue
        if og >= 2:
            emit_ffn1_evac(og - 2)
        for half in range(2):
            wl = [(sV, prep_v[og])]
            wl.append((sV, qz_half if (half == 0 and og == 0) else qz1))
            if half == 1 and og == 0:
                wl = [(sV, prep_v[og]), (sV, qz1)]
            if half == 0 and og >= 2:
                wl.append((sV, ffn1_evac[og - 2]))
            def f(t, og=og, half=half):
                last = None
                for dc in range(8 * half, 8 * half + 8):
                    st, sp = dc == 0, dc == NCD - 1
                    lt = wblk4(og)[:, 128 * dc: 128 * (dc + 1)]
                    t.matmul(psS[og % 2][:, 0:512], lt, actc(dc, 0), start=st, stop=sp)
                    last = t.matmul(psS[og % 2][:, 512:1024], lt, actc(dc, 1), start=st, stop=sp)
                last.then_inc(sPE, 1)
            P.add("tensor", f, wait=wl)
            bump(sPE, 1)
        ffn1_mm.append(P.cnt[sPE.name])

        def emit_ffn1_evac(og2):
            def f(v, og2=og2):
                ps = psS[og2 % 2]
                v.scalar_tensor_tensor(x_ch(og2, 0), ps[:, 0:512], 0.25, x_ch(og2, 0),
                                       op0=OP.mult, op1=OP.add)
                v.scalar_tensor_tensor(x_ch(og2, 1), ps[:, 512:1024], 0.25, x_ch(og2, 1),
                                       op0=OP.mult, op1=OP.add).then_inc(sV, 1)
            P.add("vector", f, wait=[(sPE, ffn1_mm[og2])])
            ffn1_evac.append(bump(sV, 1))

    if stage != 12:
        emit_ffn1_evac(NCD - 2)
        emit_ffn1_evac(NCD - 1)
    es0.close()
    aux_stacks.remove(es0)

    if stage == 12:
        return finish([(sV, prep_v[-1]), (sQ, P.cnt[sQ.name])])
    if stage == 0:
        return finish([(sV, ffn1_evac[-1])])

    es1 = ExitStack()
    aux_stacks.append(es1)
    c1 = es1.enter_context(nc.sbuf_tensor("c1", [128, NCD * T], BF16))
    bvl_bc = es1.enter_context(nc.sbuf_tensor("bvl_bc", [128, 1024], F32))
    tln = es1.enter_context(nc.sbuf_tensor("tln", [128, 512], F32))
    c1c = lambda ic, th: c1[:, T * ic + 512 * th: T * ic + 512 * th + 512]

    bc2 = emit_ln_stats(lambda dc: [(sV, ffn1_evac[dc])], "inv")
    xh_done = []
    for dc in range(NCD):
        for th in range(2):
            def f(v, dc=dc, th=th):
                sl = slice(512 * th, 512 * (th + 1))
                v.tensor_tensor(tln[:, :], x_ch(dc, th), psC[:, sl], op=OP.subtract)
                v.tensor_tensor(actc(dc, th), tln[:, :], psD[:, sl], op=OP.mult).then_inc(sV, 1)
            P.add("vector", f, wait=[bc2])
            xh_done.append(bump(sV, 1))
    xh_all = xh_done[-1]

    def f(sy):
        sy.dma_start(out=thrm_row[:, :], in_=APx(din["bvl"], 0, [[1024, 1], [1, 1024]])).then_inc(sQ, 16)
    P.add("sync", f, wait=[bc1])
    bvl_q = bump(sQ, 16)

    def wk_ap(name, oc):
        return APx(din[name], oc * 128, [[D, 128], [128 * D, 16], [1, 128]])

    def wl_ap(name, dlc):
        return APx(din[name], dlc * 128, [[DL, 128], [128 * DL, 16], [1, 128]])

    def emit_proj(n_in, n_out, rhs_col, wt_ap_fn, evac_fn, gate0):
        ld, mm, ev = [], [], []
        for oc in range(n_out):
            s = oc % 2
            wl = list(gate0) if oc < 2 else [(sPE, mm[oc - 2])]
            def f(gp, oc=oc, s=s):
                gp.dma_start(out=wblk(s)[:, 0:128 * n_in], in_=wt_ap_fn(oc)).then_inc(sG, 16)
            P.add("gpsimd", f, wait=wl)
            ld.append(bump(sG, 16))

            wl = [(sG, ld[oc])]
            if oc >= 2:
                wl.append(ev[oc - 2])
            def f(t, oc=oc, s=s):
                ps = psS[oc % 2]
                last = None
                for ic in range(n_in):
                    st, sp = ic == 0, ic == n_in - 1
                    lt = wblk(s)[:, 128 * ic: 128 * (ic + 1)]
                    t.matmul(ps[:, 0:512], lt, rhs_col(ic, 0), start=st, stop=sp)
                    last = t.matmul(ps[:, 512:1024], lt, rhs_col(ic, 1), start=st, stop=sp)
                last.then_inc(sPE, 1)
            P.add("tensor", f, wait=wl)
            mm.append(bump(sPE, 1))
            ev.append(evac_fn(oc, mm[oc]))
        return mm, ev

    def evac_c1(bias_sb):
        def ev(oc, mmc):
            def f(sc, oc=oc):
                ps = psS[oc % 2]
                sc.activation(c1c(oc, 0), ps[:, 0:512], AF.Identity, bias=bias_sb[:, oc:oc + 1])
                sc.activation(c1c(oc, 1), ps[:, 512:1024], AF.Identity,
                              bias=bias_sb[:, oc:oc + 1]).then_inc(sA, 1)
            P.add("scalar", f, wait=[(sPE, mmc)])
            return (sA, bump(sA, 1))
        return ev

    kq_dma = []

    def evac_kq(bias_sb, sect):
        def ev(dlc, mmc):
            s = dlc % 2
            wl = [(sPE, mmc)]
            if len(kq_dma) >= 2:
                wl.append((sQ, kq_dma[-2]))
            def f(sc, dlc=dlc, s=s):
                ps = psS[dlc % 2]
                sc.activation(kslot(s)[:, 0:512], ps[:, 0:512], AF.Identity, bias=bias_sb[:, dlc:dlc + 1])
                sc.activation(kslot(s)[:, 512:1024], ps[:, 512:1024], AF.Identity,
                              bias=bias_sb[:, dlc:dlc + 1]).then_inc(sA, 1)
            P.add("scalar", f, wait=wl)
            a = bump(sA, 1)
            def f(sy, dlc=dlc, s=s, sect=sect):
                sy.dma_start(out=APx(ccain, (sect * 8 + dlc) * CH, [[1024, 128], [1, 1024]]),
                             in_=kslot(s)[:, :]).then_inc(sQ, 16)
            P.add("sync", f, wait=[(sA, a)])
            kq_dma.append(bump(sQ, 16))
            return (sA, a)
        return ev

    g_ffn1 = [(sPE, ffn1_mm[-1]), (sV, xh_all)]
    k1_mm, k1_ev = emit_proj(NCD, NCD, actc, lambda oc: wk_ap("wk", oc),
                             evac_c1(bk_sb), g_ffn1)
    k2_mm, k2_ev = emit_proj(NCD, NDL, c1c, lambda oc: wl_ap("wkl", oc),
                             evac_kq(bkl_sb, 0), [k1_ev[-1]])
    q1_mm, q1_ev = emit_proj(NCD, NCD, actc, lambda oc: wk_ap("wq", oc),
                             evac_c1(bq_sb), [k2_ev[-1]])
    q2_mm, q2_ev = emit_proj(NCD, NDL, c1c, lambda oc: wl_ap("wql", oc),
                             evac_kq(bql_sb, 1), [q1_ev[-1]])
    kq_dma_all = kq_dma[-1]

    v1_mm, v1_ev = emit_proj(NCD, NCD, actc, lambda oc: wk_ap("wv", oc),
                             evac_c1(bv_sb), [q2_ev[-1]])

    def f(v):
        v.tensor_copy(scrB[0:1, 0:1024], thrm_row[0:1, :])
        v.tensor_tensor(scrB[0:1, 1024:2048], thrm_row[0:1, :],
                        scrB[0:1, 0:1024], op=OP.subtract).then_inc(sV, 1)
    P.add("vector", f, wait=[(sQ, bvl_q), (sV, xh_all)])
    bvl_v = bump(sV, 1)

    def f(t):
        ob = ones_b2[0:1, 0:128]
        last = None
        for th in range(2):
            sl = slice(512 * th, 512 * (th + 1))
            t.matmul(psD[:, sl], ob, scrB[0:1, 512 * th:512 * (th + 1)],
                     start=True, stop=False)
            last = t.matmul(psD[:, sl], ob, scrB[0:1, 1024 + 512 * th:1024 + 512 * (th + 1)],
                     start=False, stop=True)
        last.then_inc(sPE, 1)
    P.add("tensor", f, wait=[(sV, bvl_v)])
    bvl_pe = bump(sPE, 1)

    def f(sc):
        sc.activation(bvl_bc[:, :], psD[:, :], AF.Copy).then_inc(sA, 1)
    P.add("scalar", f, wait=[(sPE, bvl_pe)])
    bvl_a = bump(sA, 1)

    wvl_ld, vmm, v_ev, vdma = [], [], [], []
    for tcg in range(2):
        for oc in range(NCD):
            s = oc % 2
            if tcg == 0 and oc < 2:
                wl = [v1_ev[-1], (sA, bvl_a)]
            elif len(vmm) >= 2:
                wl = [(sPE, vmm[-1])]
            else:
                wl = []
            def f(gp, oc=oc, s=s):
                gp.dma_start(out=wvlt(s)[:, :],
                             in_=APx(din["wvl"], oc * 128 * DL, [[DL, 128], [1, DL]])).then_inc(sG, 16)
            P.add("gpsimd", f, wait=wl)
            wvl_ld.append(bump(sG, 16))

            wl = [(sG, wvl_ld[-1])]
            if tcg == 1 and oc == 0:
                wl.append(v_ev[-1])
            def f(t, tcg=tcg, oc=oc, s=s):
                st, sp = oc == 0, oc == NCD - 1
                last = None
                for j in range(4):
                    tc = 4 * tcg + j
                    lt = c1[:, T * oc + 128 * tc: T * oc + 128 * (tc + 1)]
                    t.matmul(psALL[j][:, 0:512], lt, wvlt(s)[:, 0:512], start=st, stop=sp)
                    last = t.matmul(psALL[j][:, 512:1024], lt, wvlt(s)[:, 512:1024], start=st, stop=sp)
                last.then_inc(sPE, 1)
            P.add("tensor", f, wait=wl)
            vmm.append(bump(sPE, 1))

        for j in range(4):
            tc = 4 * tcg + j
            s = tc % 2
            wl = [(sPE, vmm[-1])]
            if len(vdma) >= 2:
                wl.append((sQ, vdma[-2]))
            def f(v, j=j, s=s):
                v.tensor_tensor(vslot(s)[:, 0:512], psALL[j][:, 0:512], bvl_bc[:, 0:512], op=OP.add)
                v.tensor_tensor(vslot(s)[:, 512:1024], psALL[j][:, 512:1024],
                                bvl_bc[:, 512:1024], op=OP.add).then_inc(sV, 1)
            P.add("vector", f, wait=wl)
            v_ev.append((sV, bump(sV, 1)))
            def f(sy, tc=tc, s=s):
                sy.dma_start(out=APx(ccain, (16 + tc) * CH, [[1024, 128], [1, 1024]]),
                             in_=vslot(s)[:, :]).then_inc(sQ, 16)
            P.add("sync", f, wait=[v_ev[-1]])
            vdma.append(bump(sQ, 16))
    v_ev_last = v_ev[-1][1]

    def f(gp):
        gp.collective_compute("AllGather", OP.bypass, replica_groups=GROUPS,
                              ins=[ccain.ap().opt()],
                              outs=[ccaout.ap().opt()]).then_inc(sCC, 1)
    P.add("gpsimd", f, wait=[(sQ, vdma[-1])])
    cc_v = bump(sCC, 1)

    proj_all_mm = P.cnt[sPE.name]

    if stage == 13:
        tgt13 = P.cnt[sG.name] + 16 * 7
        def f(gp, tgt13=tgt13):
            gp.dma_start(out=dbg["d_xh"].ap(), in_=act_bf[:, :]).then_inc(sG, 16)
            gp.dma_start(out=dbg["d_c1"].ap(), in_=c1[:, :]).then_inc(sG, 16)
            gp.dma_start(out=dbg["d_bvl"].ap(), in_=bvl_bc[:, :]).then_inc(sG, 16)
            for i in range(4):
                gp.dma_start(out=APx(dbg["d_ccv2"], i * 2 * CH, [[1, 1], [1, 2 * CH]]),
                             in_=APx(ccain, 16 * CH + i * 2 * CH, [[1, 1], [1, 2 * CH]])).then_inc(sG, 16)
            gp.wait_ge(sG, tgt13)
        P.add("gpsimd", f, wait=[(sCC, cc_v), (sPE, proj_all_mm)])
        bump(sG, 16 * 7)
        def f(gp):
            gp.nop().then_inc(sG, 1)
        P.add("gpsimd", f)
        gv13 = bump(sG, 1)
        return finish([(sG, gv13)])

    if stage == 1:
        def f(gp):
            gp.nop().then_inc(sG, 1)
        P.add("gpsimd", f, wait=[(sCC, cc_v)])
        gv = bump(sG, 1)
        return finish([(sV, xh_all), (sG, gv)])

    es1.close()
    aux_stacks.remove(es1)
    es3 = ExitStack()
    aux_stacks.append(es3)
    k_att = [es3.enter_context(nc.sbuf_tensor(f"k_att{p}", [128, TL], BF16)) for p in range(2)]
    q_att = [es3.enter_context(nc.sbuf_tensor(f"q_att{p}", [128, TL], BF16)) for p in range(2)]
    v_aug = [es3.enter_context(nc.sbuf_tensor(f"v_aug{p}", [128, 32 * 130], BF16)) for p in range(2)]
    rr_bf = es3.enter_context(nc.sbuf_tensor("rr_bf", [33, 512], BF16))

    att_ld = []
    for p in range(2):
        wl = [(sCC, cc_v)]
        if p == 0:
            wl.append((sPE, proj_all_mm))
        def f(gp, p=p):
            gp.memset(v_aug[p][:, :], 1.0)
            gp.reg_load(rg0, tbl_sb[0:1, p:p + 1])
            gp.dma_start(out=APx(k_att[p], 0, [[TL, 128], [1024, 4], [1, 1024]]),
                         in_=bass.AP(ccaout, rg0, [[1024, 128], [B_A, 4], [1, 1024]])).then_inc(sG, 16)
            gp.reg_add(rg1, rg0, 8 * CH)
            gp.dma_start(out=APx(q_att[p], 0, [[TL, 128], [1024, 4], [1, 1024]]),
                         in_=bass.AP(ccaout, rg1, [[1024, 128], [B_A, 4], [1, 1024]])).then_inc(sG, 16)
            gp.reg_load(rg0, tbl_sb[0:1, 2 + p:3 + p])
            for r in range(4):
                for h in range(2):
                    gp.reg_add(rg1, rg0, r * B_A + 16 * CH + 64 * h)
                    gp.dma_start(out=APx(v_aug[p], r * 8 * 130 + 65 * h,
                                         [[32 * 130, 128], [130, 8], [1, 64]]),
                                 in_=bass.AP(ccaout, rg1,
                                             [[1024, 128], [CH, 8], [1, 64]])).then_inc(sG, 16)
        P.add("gpsimd", f, wait=wl)
        att_ld.append(bump(sG, 16 * 10))

    def wo_ap(oc):
        return APx(din["wo"], oc * 128, [[D, 128], [128 * D, 8], [1, 128]])
    wo_ld = []
    for oc in range(2):
        def f(gp, oc=oc):
            gp.dma_start(out=wot(oc % 2)[:, :], in_=wo_ap(oc)).then_inc(sG, 16)
        P.add("gpsimd", f, wait=[(sCC, cc_v)])
        wo_ld.append(bump(sG, 16))

    ho_mult_all = []
    rc_v_all = []
    for p in range(2):
        wl = [(sG, att_ld[p])]
        if p == 1:
            wl.append((sV, ho_mult_all[-1]))
        def f(sc, p=p):
            sc.activation(kq2v[:, :], k_att[p][:, :], AF.Square).then_inc(sA, 1)
        P.add("scalar", f, wait=wl)
        a_sq = bump(sA, 1)

        nrm_pe, nrm_v = [], []
        for which in range(2):
            if which == 1:
                def f(sc, p=p):
                    sc.activation(kq2v[:, :], q_att[p][:, :], AF.Square).then_inc(sA, 1)
                P.add("scalar", f, wait=[(sV, nrm_v[-1])])
                a_sq = bump(sA, 1)
            for c in range(8):
                bs = 512 * (c % 2)
                wl = [(sA, a_sq)]
                if c >= 2:
                    wl.append((sV, nrm_v[-2]))
                if p == 0 and which == 0 and c < 2:
                    wl.append((sV, v_ev_last))
                if p == 1 and which == 0 and c < 2:
                    wl.append((sV, ho_mult_all[-1]))
                def f(t, c=c, bs=bs):
                    t.matmul(psD[0:1, bs:bs + 512], ones_bf[:, 0:1],
                             kq2v[:, 512 * c: 512 * (c + 1)], start=True, stop=True).then_inc(sPE, 1)
                P.add("tensor", f, wait=wl)
                nrm_pe.append(bump(sPE, 1))
                def f(v, c=c, bs=bs, which=which):
                    dst = sc_row[0:1, which:which + 1]
                    if c == 0:
                        v.reduce_max(dst, psD[0:1, bs:bs + 512], axis=AXX).then_inc(sV, 1)
                    else:
                        v.reduce_max(sc_row[0:1, 2:3], psD[0:1, bs:bs + 512], axis=AXX)
                        v.tensor_tensor(dst, dst, sc_row[0:1, 2:3], op=OP.max).then_inc(sV, 1)
                P.add("vector", f, wait=[(sPE, nrm_pe[-1])])
                nrm_v.append(bump(sV, 1))

        def f(v):
            v.tensor_tensor(gq_tmp[0:1, 0:1], sc_row[0:1, 0:1], sc_row[0:1, 1:2], op=OP.add)
            v.tensor_copy(sc_row[0:1, 3:4], gq_tmp[0:1, 0:1])
            v.tensor_copy(sc_row[0:1, 4:5], sc_row[0:1, 3:4]).then_inc(sV, 1)
        P.add("vector", f, wait=[(sV, nrm_v[-1])])
        nb_v = bump(sV, 1)
        def f(t):
            t.matmul(psD[:, 0:1], negr[0:1, 0:128], gq_tmp[0:1, 0:1],
                     start=True, stop=True).then_inc(sPE, 1)
        P.add("tensor", f, wait=[(sV, nb_v)])
        nb_pe = bump(sPE, 1)
        def f(sc, p=p):
            sc.activation(bias4[:, 3 + p:4 + p], psD[:, 0:1], AF.Copy).then_inc(sA, 1)
        P.add("scalar", f, wait=[(sPE, nb_pe)])
        nb_a = bump(sA, 1)

        groups = [(qt, kb, grp) for qt in range(8) for kb in range(qt + 1) for grp in range(2)]
        G = len(groups)
        PT = lambda s: 2048 * s
        s_inc, exp_done, mask_v, av_inc = {}, {}, {}, {}
        qt_last_g = {qt: max(g for g, (q_, _, _) in enumerate(groups) if q_ == qt)
                     for qt in range(8)}
        pend_bcast = None

        def emit_S(g):
            qt, kb, grp = groups[g]
            wl = []
            if g == 0:
                wl.append((sA, nb_a))
                if p == 0:
                    wl.append((sV, v_ev_last))
            else:
                wl.append((sA, exp_done[g - 1]))
            def f(t, p=p, qt=qt, kb=kb, grp=grp):
                last = None
                for kloc in range(2):
                    gblk = 4 * kb + 2 * grp + kloc
                    for h in range(2):
                        last = t.matmul(
                            psAB[:, 512 * (2 * kloc + h): 512 * (2 * kloc + h) + 512],
                            k_att[p][64 * h: 64 * h + 64, 128 * gblk: 128 * (gblk + 1)],
                            q_att[p][64 * h: 64 * h + 64, 512 * qt: 512 * (qt + 1)],
                            start=True, stop=True, tile_position=(64 * h, 0))
                last.then_inc(sPE, 1)
            P.add("tensor", f, wait=wl)
            s_inc[g] = bump(sPE, 1)

        def emit_exp(g):
            s = g % 2
            def f(sc, p=p, s=s):
                sc.activation(scrB[:, PT(s): PT(s) + 2048], psAB[:, 0:2048],
                              AF.Exp, bias=bias4[:, 3 + p:4 + p],
                              scale=0.125).then_inc(sA, 1)
            P.add("scalar", f, wait=[(sPE, s_inc[g])])
            exp_done[g] = bump(sA, 1)

        def emit_masks(g):
            qt, kb, grp = groups[g]
            s = g % 2
            def f(v, s=s, grp=grp):
                for kloc in range(2):
                    i = 2 * grp + kloc
                    for h in range(2):
                        c0 = PT(s) + 512 * (2 * kloc + h)
                        v.tensor_tensor(scrB[:, c0:c0 + 512], scrB[:, c0:c0 + 512],
                                        mk4[:, 512 * i:512 * (i + 1)], op=OP.mult)
                v.tensor_copy(lq_tmp[0:1, 0:8],
                              scrB[0:1, PT(s):PT(s) + 8]).then_inc(sV, 1)
            P.add("vector", f, wait=[(sA, exp_done[g])])
            mask_v[g] = bump(sV, 1)

        def emit_AV(g):
            qt, kb, grp = groups[g]
            s = g % 2
            st = (kb == 0 and grp == 0)
            sp = (kb == qt and grp == 1)
            avoff = 512 * (qt % 2)
            wl = [(sA, exp_done[g])]
            if g in mask_v:
                wl.append((sV, mask_v[g]))
            if st:
                if len(ho_mult_all) >= 2:
                    wl.append((sV, ho_mult_all[-2]))
                if rc_v_all:
                    wl.append((sV, rc_v_all[-1]))
            def f(t, p=p, kb=kb, grp=grp, s=s, st=st, sp=sp, avoff=avoff):
                for kloc in range(2):
                    gblk = 4 * kb + 2 * grp + kloc
                    pt0 = scrB[:, PT(s) + 512 * (2 * kloc): PT(s) + 512 * (2 * kloc) + 512]
                    pt1 = scrB[:, PT(s) + 512 * (2 * kloc + 1): PT(s) + 512 * (2 * kloc + 1) + 512]
                    t.matmul(psC[0:64, avoff:avoff + 512],
                             v_aug[p][:, 130 * gblk: 130 * gblk + 64],
                             pt0, start=st and kloc == 0, stop=sp and kloc == 1,
                             tile_position=(0, 0))
                    t.matmul(psC[64:128, avoff:avoff + 512],
                             v_aug[p][:, 130 * gblk + 65: 130 * gblk + 129],
                             pt1, start=st and kloc == 0, stop=sp and kloc == 1,
                             tile_position=(0, 64))
                    t.matmul(psD[0:1, 0:512],
                             v_aug[p][:, 130 * gblk + 64: 130 * gblk + 65],
                             pt0, start=st and kloc == 0, stop=sp and kloc == 1,
                             tile_position=(0, 0))
                    last = t.matmul(psD[32:33, 0:512],
                                    v_aug[p][:, 130 * gblk + 129: 130 * gblk + 130],
                                    pt1, start=st and kloc == 0, stop=sp and kloc == 1,
                                    tile_position=(0, 32))
                last.then_inc(sPE, 1)
            P.add("tensor", f, wait=wl)
            av_inc[g] = bump(sPE, 1)

        def emit_recip(qt):
            def f(v):
                with nc.allow_low_precision(reason="bf16 softmax denominator recip"):
                    v.reciprocal(rr_bf[0:1, 0:512], psD[0:1, 0:512])
                    v.reciprocal(rr_bf[32:33, 0:512], psD[32:33, 0:512]).then_inc(sV, 1)
            P.add("vector", f, wait=[(sPE, av_inc[qt_last_g[qt]])])
            rc_v_all.append(bump(sV, 1))

        def emit_bcast_and_evac(qt):
            def f(t):
                t.matmul(psD[0:64, 512:1024], ones_b2[0:1, 0:64], rr_bf[0:1, 0:512],
                         start=True, stop=True, tile_position=(0, 0))
                t.matmul(psD[64:128, 512:1024], ones_b2[32:33, 0:64], rr_bf[32:33, 0:512],
                         start=True, stop=True, tile_position=(32, 64)).then_inc(sPE, 1)
            P.add("tensor", f, wait=[(sV, rc_v_all[-1])])
            bc_pe = bump(sPE, 1)
            def f(v, p=p, qt=qt):
                v.tensor_copy(bcB[:, 0:512], psD[:, 512:1024])
                hob = ho_off(p) + 512 * qt
                avoff = 512 * (qt % 2)
                v.tensor_tensor(scrB[0:64, hob:hob + 512],
                                psC[0:64, avoff:avoff + 512], bcB[0:64, 0:512], op=OP.mult)
                v.tensor_tensor(scrB[64:128, hob:hob + 512],
                                psC[64:128, avoff:avoff + 512], bcB[64:128, 0:512],
                                op=OP.mult).then_inc(sV, 1)
            P.add("vector", f, wait=[(sPE, bc_pe)])
            ho_mult_all.append(bump(sV, 1))

        for g in range(G + 1):
            if g < G:
                emit_S(g)
            if pend_bcast is not None and g >= pend_bcast[1] + 1:
                emit_bcast_and_evac(pend_bcast[0])
                pend_bcast = None
            if g >= 1:
                emit_AV(g - 1)
                qtp, kbp, grpp = groups[g - 1]
                if g - 1 == qt_last_g[qtp]:
                    emit_recip(qtp)
                    pend_bcast = (qtp, g)
            if g < G:
                emit_exp(g)
                if groups[g][1] == groups[g][0]:
                    emit_masks(g)
        if pend_bcast is not None:
            emit_bcast_and_evac(pend_bcast[0])
            pend_bcast = None

        def f(sy, p=p):
            sy.dma_start(out=APx(cc2_in, p * 128 * TL, [[TL, 128], [1, TL]]),
                         in_=APx(scrB, ho_off(p), [[16384, 128], [1, TL]])).then_inc(sQ, 16)
        P.add("sync", f, wait=[(sV, ho_mult_all[-1])])
        bump(sQ, 16)
    cc2_wr = P.cnt[sQ.name]
    att_pe_all = P.cnt[sPE.name]

    if stage == 20:
        tgt20 = P.cnt[sG.name] + 128
        def f(gp, tgt20=tgt20):
            gp.dma_start(out=dbg["d_k0"].ap(), in_=k_att[0][:, :]).then_inc(sG, 16)
            gp.dma_start(out=dbg["d_q0"].ap(), in_=q_att[0][:, :]).then_inc(sG, 16)
            gp.dma_start(out=dbg["d_va0"].ap(), in_=v_aug[0][:, :]).then_inc(sG, 16)
            gp.dma_start(out=dbg["d_ho"].ap(), in_=scrB[:, 6144:14336]).then_inc(sG, 16)
            gp.dma_start(out=dbg["d_bias"].ap(), in_=bias4[:, :]).then_inc(sG, 16)
            gp.dma_start(out=APx(dbg["d_ccv"], 0, [[1, 1], [1, B_V]]),
                         in_=APx(ccain, 16 * CH, [[1, 1], [1, B_V]])).then_inc(sG, 16)
            gp.dma_start(out=APx(dbg["d_scrow"], 0, [[16, 1], [1, 16]]), in_=sc_row[:, :]).then_inc(sG, 16)
            gp.dma_start(out=dbg["d_kq2"].ap(), in_=kq2v).then_inc(sG, 16)
            gp.wait_ge(sG, tgt20)
        P.add("gpsimd", f, wait=[(sQ, cc2_wr)])
        bump(sG, 128)

    def f(gp):
        gp.collective_compute("AllGather", OP.bypass, replica_groups=GROUPS,
                              ins=[cc2_in.ap().opt()],
                              outs=[cc2_out.ap().opt()]).then_inc(sCC, 1)
    P.add("gpsimd", f, wait=[(sQ, cc2_wr)])
    cc_2 = bump(sCC, 1)

    es3.close()
    aux_stacks.remove(es3)
    es4 = ExitStack()
    aux_stacks.append(es4)
    ho_all = es4.enter_context(nc.sbuf_tensor("ho_all", [128, NDL * 1024], BF16))
    hoc = lambda ic, th: ho_all[:, 1024 * ic + 512 * th: 1024 * ic + 512 * th + 512]

    def f(gp):
        gp.reg_load(rg0, tbl_sb[0:1, 4:5])
        gp.dma_start(out=APx(ho_all, 0, [[NDL * 1024, 128], [1024, 8], [1, 1024]]),
                     in_=bass.AP(cc2_out, rg0, [[TL, 128], [128 * TL, 8], [1, 1024]])).then_inc(sG, 16)
    P.add("gpsimd", f, wait=[(sCC, cc_2)])
    ho_ld = bump(sG, 16)

    wo_mm, wo_ev = [], []
    for oc in range(NCD):
        s = oc % 2
        if oc >= 2:
            def f(gp, oc=oc, s=s):
                gp.dma_start(out=wot(s)[:, :], in_=wo_ap(oc)).then_inc(sG, 16)
            P.add("gpsimd", f, wait=[(sPE, wo_mm[oc - 2])])
            wo_ld.append(bump(sG, 16))

        wl = [(sG, wo_ld[oc]), (sG, ho_ld)]
        if oc >= 2:
            wl.append((sV, wo_ev[oc - 2]))
        def f(t, s=s):
            ps = psS[s]
            last = None
            for ic in range(NDL):
                st, sp = ic == 0, ic == NDL - 1
                lt = wot(s)[:, 128 * ic: 128 * (ic + 1)]
                t.matmul(ps[:, 0:512], lt, hoc(ic, 0), start=st, stop=sp)
                last = t.matmul(ps[:, 512:1024], lt, hoc(ic, 1), start=st, stop=sp)
            last.then_inc(sPE, 1)
        P.add("tensor", f, wait=wl)
        wo_mm.append(bump(sPE, 1))

        def f(v, oc=oc, s=s):
            ps = psS[s]
            v.scalar_tensor_tensor(x_ch(oc, 0), ps[:, 0:512], bo_sb[:, oc:oc + 1],
                                   x_ch(oc, 0), op0=OP.add, op1=OP.add)
            v.scalar_tensor_tensor(x_ch(oc, 1), ps[:, 512:1024], bo_sb[:, oc:oc + 1],
                                   x_ch(oc, 1), op0=OP.add, op1=OP.add).then_inc(sV, 1)
        P.add("vector", f, wait=[(sPE, wo_mm[oc])])
        wo_ev.append(bump(sV, 1))

    if stage == 20:
        tgt21 = P.cnt[sG.name] + 16
        def f(gp, tgt21=tgt21):
            gp.dma_start(out=dbg["d_hoall"].ap(), in_=ho_all[:, :]).then_inc(sG, 16)
            gp.wait_ge(sG, tgt21)
        P.add("gpsimd", f, wait=[(sG, tgt21 - 16)])
        bump(sG, 16)
    if stage in (2, 20):
        return finish([(sV, wo_ev[-1])])

    bc3 = emit_ln_stats(lambda dc: [(sV, wo_ev[dc])], "thr")
    qz3, qz3_half = emit_quantize(bc3)

    ffn2_mm, ffn2_ev = [], []
    for og in range(NCD):
        s = og % 2
        wl = [(sG, ho_ld)] if og < 2 else [(sPE, ffn2_mm[og - 2])]
        def f(sy, og=og, s=s):
            sy.dma_start(out=wblk(s)[:, :],
                         in_=APx(wr2_hbm, og * 128 * D, [[D, 128], [1, D]])).then_inc(sQ, 16)
        P.add("sync", f, wait=wl)
        ld = bump(sQ, 16)

        for half in range(2):
            wl = [(sQ, ld)]
            wl.append((sV, qz3_half if (half == 0 and og == 0) else qz3))
            if half == 0 and og >= 2:
                wl.append((sV, ffn2_ev[og - 2]))
            def f(t, s=s, half=half):
                last = None
                for dc in range(8 * half, 8 * half + 8):
                    st, sp = dc == 0, dc == NCD - 1
                    lt = wblk(s)[:, 128 * dc: 128 * (dc + 1)]
                    t.matmul(psS[s][:, 0:512], lt, actc(dc, 0), start=st, stop=sp)
                    last = t.matmul(psS[s][:, 512:1024], lt, actc(dc, 1), start=st, stop=sp)
                last.then_inc(sPE, 1)
            P.add("tensor", f, wait=wl)
            bump(sPE, 1)
        ffn2_mm.append(P.cnt[sPE.name])

        def f(v, og=og, s=s):
            ps = psS[s]
            v.scalar_tensor_tensor(x_ch(og, 0), ps[:, 0:512], 0.25, x_ch(og, 0),
                                   op0=OP.mult, op1=OP.add)
            v.scalar_tensor_tensor(x_ch(og, 1), ps[:, 512:1024], 0.25, x_ch(og, 1),
                                   op0=OP.mult, op1=OP.add).then_inc(sV, 1)
        P.add("vector", f, wait=[(sPE, ffn2_mm[og])])
        ffn2_ev.append(bump(sV, 1))

    def f(sy):
        for i in range(4):
            sy.dma_start(
                out=APx(out_t, 4 * T * 128 * i, [[T, 128], [128 * T, 4], [1, T]]),
                in_=APx(x_res, 4 * T * i, [[NCD * T, 128], [T, 4], [1, T]]),
            ).then_inc(sQ, 16)
    P.add("sync", f, wait=[(sV, ffn2_ev[-1])])
    out_done = bump(sQ, 64)

    def f(sy):
        pass
    P.add("sync", f, wait=[(sQ, out_done)])

    with nc.Block() as block:
        P.realize(block)
    es4.close()
    es.close()
    return nc


_NC_CACHE = None


def kernel(__stage=99, **inputs):
    global _NC_CACHE
    x = np.asarray(inputs["L0_attn_x"], dtype=np.float32)
    W = {k: np.asarray(v, dtype=np.float32) for k, v in inputs.items() if k != "L0_attn_x"}

    if _NC_CACHE is None or _NC_CACHE[0] != __stage:
        _NC_CACHE = (__stage, build_nc(stage=__stage))
    nc = _NC_CACHE[1]

    shared = {
        "wk": np.ascontiguousarray(W["L0_attn_wk"]),
        "wq": np.ascontiguousarray(W["L0_attn_wq"]),
        "wv": np.ascontiguousarray(W["L0_attn_wv"]),
        "wkl": np.ascontiguousarray(W["L0_attn_wkl"]),
        "wql": np.ascontiguousarray(W["L0_attn_wql"]),
        "wvl": np.ascontiguousarray(W["L0_attn_wvl"]),
        "wo": np.ascontiguousarray(W["L0_attn_wo"]),
        "bk": np.ascontiguousarray(W["L0_attn_bk"]),
        "bq": np.ascontiguousarray(W["L0_attn_bq"]),
        "bv": np.ascontiguousarray(W["L0_attn_bv"]),
        "bo": np.ascontiguousarray(W["L0_attn_bo"]),
        "bkl": np.ascontiguousarray(W["L0_attn_bkl"]),
        "bql": np.ascontiguousarray(W["L0_attn_bql"]),
        "bvl": np.ascontiguousarray(W["L0_attn_bvl"]),
        "fwt": np.ascontiguousarray(W["L0_ffn_weight"].T),
        "frt": np.ascontiguousarray(W["L0_ffn_rram_variation"].T),
    }
    in_maps = []
    for c in range(8):
        b, qq = divmod(c, 4)
        m = dict(shared)
        m["xt"] = np.ascontiguousarray(x[b, T * qq: T * (qq + 1), :].T)
        hc = 2 * qq
        m["tbl"] = np.array([[hc * CH, (hc + 1) * CH, 128 * hc, 128 * (hc + 1),
                              qq * 1024, 0, 0, 0]], dtype=np.int32)
        in_maps.append(m)

    res = run_bass_kernel_spmd(nc, in_maps, core_ids=list(range(8)))
    out = np.empty((2, TL, D), dtype=np.float32)
    for c in range(8):
        b, qq = divmod(c, 4)
        out[b, T * qq: T * (qq + 1), :] = res.results[c]["out_t"].T
    return out


# version 07
# speedup vs baseline: 1.1851x; 1.1851x over previous
import numpy as np
import ml_dtypes
from contextlib import ExitStack

import concourse.bass as bass
import concourse.mybir as mybir
from concourse.bass_utils import run_bass_kernel_spmd

F32 = mybir.dt.float32
BF16 = mybir.dt.bfloat16
I32 = mybir.dt.int32
AF = mybir.ActivationFunctionType
OP = mybir.AluOpType
AXX = mybir.AxisListType.X

D = 2048
T = 1024
TL = 4096
DL = 1024
NCD = 16
NDL = 8
CH = 128 * 1024
B_KQ = 16 * CH
B_V = 8 * CH
B_A = 24 * CH
B_HO = 2 * 128 * TL
GROUPS = [[0, 1, 2, 3], [4, 5, 6, 7]]


def APx(t, off, dims):
    return bass.AP(t, off, [list(d) for d in dims])


class Prog:
    ENGINES = ("tensor", "vector", "scalar", "sync", "gpsimd")

    def __init__(self):
        self.ops = {e: [] for e in self.ENGINES}
        self.cnt = {}

    def add(self, eng, fn, wait=()):
        self.ops[eng].append(([(s, v) for (s, v) in wait], fn))

    def bump(self, sem, n=1):
        self.cnt[sem.name] = self.cnt.get(sem.name, 0) + n
        return self.cnt[sem.name]

    def realize(self, block):
        def runner(ename):
            def run(eng):
                for waits, fn in self.ops[ename]:
                    for s, v in waits:
                        eng.wait_ge(s, v)
                    fn(eng)
            return run
        block.tensor(runner("tensor"))
        block.vector(runner("vector"))
        block.scalar(runner("scalar"))
        block.sync(runner("sync"))
        block.gpsimd(runner("gpsimd"))


def build_nc(stage=99):
    nc = bass.Bass("TRN2", target_bir_lowering=False)
    es = ExitStack()

    din = {}
    for name, shape in [("xt", [D, T]), ("wk", [D, D]), ("wq", [D, D]), ("wv", [D, D]),
                        ("wkl", [D, DL]), ("wql", [D, DL]), ("wvl", [D, DL]),
                        ("wo", [DL, D]), ("fwt", [D, D]), ("frt", [D, D])]:
        din[name] = nc.dram_tensor(name, shape, F32, kind="ExternalInput")
    for name, n in [("bk", D), ("bq", D), ("bv", D), ("bo", D),
                    ("bkl", DL), ("bql", DL), ("bvl", DL)]:
        din[name] = nc.dram_tensor(name, [n], F32, kind="ExternalInput")
    din["tbl"] = nc.dram_tensor("tbl", [1, 8], I32, kind="ExternalInput")
    out_t = nc.dram_tensor("out_t", [D, T], F32, kind="ExternalOutput")
    dbg = {}
    if stage == 13:
        for nm, sh, dt in [("d_xh", [128, NCD * T], BF16), ("d_c1", [128, NCD * T], BF16),
                           ("d_bvl", [128, 1024], F32), ("d_ccv2", [1, B_V], BF16)]:
            dbg[nm] = nc.dram_tensor(nm, sh, dt, kind="ExternalOutput")
    if stage == 20:
        for nm, sh in [("d_k0", [128, TL]), ("d_q0", [128, TL]), ("d_va0", [128, 4160]),
                       ("d_ho", [128, 8192]), ("d_bias", [128, 8]), ("d_hoall", [128, 8192])]:
            dbg[nm] = nc.dram_tensor(nm, sh, F32, kind="ExternalOutput")
        dbg["d_ccv"] = nc.dram_tensor("d_ccv", [1, B_V], BF16, kind="ExternalOutput")
        dbg["d_scrow"] = nc.dram_tensor("d_scrow", [4, 16], F32, kind="ExternalOutput")
        dbg["d_kq2"] = nc.dram_tensor("d_kq2", [128, 4096], F32, kind="ExternalOutput")

    PAD = 4 * 1024 * 1024
    wr2_hbm = nc.dram_tensor("wr2_hbm", [NCD * 128 * D], BF16)
    nc.dram_tensor("pad0", [1, PAD], BF16)
    ccain = nc.dram_tensor("ccain", [1, B_A], BF16)
    nc.dram_tensor("pad1", [1, PAD], BF16)
    ccaout = nc.dram_tensor("ccaout", [1, 4 * B_A], BF16)
    nc.dram_tensor("pad2", [1, PAD], BF16)
    cc2_in = nc.dram_tensor("cc2_in", [1, B_HO], BF16)
    nc.dram_tensor("pad5", [1, PAD], BF16)
    cc2_out = nc.dram_tensor("cc2_out", [1, 4 * B_HO], BF16)
    nc.dram_tensor("pad6", [1, PAD], BF16)

    sQ = es.enter_context(nc.semaphore("sQ"))
    sG = es.enter_context(nc.semaphore("sG"))
    sCC = es.enter_context(nc.semaphore("sCC"))
    sPE = es.enter_context(nc.semaphore("sPE"))
    sV = es.enter_context(nc.semaphore("sV"))
    sA = es.enter_context(nc.semaphore("sA"))
    rg0 = es.enter_context(nc.gpsimd.register("rg0"))
    rg1 = es.enter_context(nc.gpsimd.register("rg1"))

    def sbt(name, shape, dt):
        return es.enter_context(nc.sbuf_tensor(name, shape, dt))
    x_res = sbt("x_res", [128, NCD * T], F32)
    ones_f = sbt("ones_f", [128, 128], F32)
    ones_bf = sbt("ones_bf", [128, 1], BF16)
    tri01 = sbt("tri01", [128, 128], BF16)
    mk4 = sbt("mk4", [128, 2048], BF16)
    tmpM = sbt("tmpM", [128, 128], F32)
    m_row = sbt("m_row", [1, 1024], F32)
    aux_row = sbt("aux_row", [1, 1024], F32)
    thrp_row = sbt("thrp_row", [1, 1024], F32)
    thrm_row = sbt("thrm_row", [1, 1024], F32)
    sc_row = sbt("sc_row", [1, 16], F32)
    bias4 = sbt("bias4", [128, 8], F32)
    bk_sb = sbt("bk_sb", [128, 16], F32)
    bq_sb = sbt("bq_sb", [128, 16], F32)
    bv_sb = sbt("bv_sb", [128, 16], F32)
    bo_sb = sbt("bo_sb", [128, 16], F32)
    bkl_sb = sbt("bkl_sb", [128, 8], F32)
    bql_sb = sbt("bql_sb", [128, 8], F32)
    tbl_sb = sbt("tbl_sb", [1, 8], I32)
    gq_tmp = sbt("gq_tmp", [128, 512], BF16)
    lq_tmp = sbt("lq_tmp", [128, 512], BF16)
    scrB = sbt("scrB", [128, 16384], BF16)
    ones_b2 = sbt("ones_b2", [128, 128], BF16)
    negr = sbt("negr", [1, 128], BF16)
    bcB = sbt("bcB", [128, 1024], F32)
    act_bf = sbt("act_bf", [128, NCD * T], BF16)

    psAB = es.enter_context(nc.psum_tensor("psAB", [128, 2048], F32))
    psC = es.enter_context(nc.psum_tensor("psC", [128, 1024], F32))
    psD = es.enter_context(nc.psum_tensor("psD", [128, 1024], F32))

    class PsView:
        def __init__(self, t, off):
            self.t, self.off = t, off

        def __getitem__(self, key):
            sp, sc = key
            return self.t[sp, self.off + sc.start: self.off + sc.stop]
    psA = PsView(psAB, 0)
    psB = PsView(psAB, 1024)
    psS = [psA, psB]
    psALL = [psA, psB, psC, psD]

    P = Prog()
    bump = P.bump
    aux_stacks = []

    xbf2 = lambda s: scrB[:, 1024 * s: 1024 * (s + 1)]
    xsq2 = lambda s: scrB[:, 2048 + 1024 * s: 3072 + 1024 * s]
    wblk = lambda s: scrB[:, 4096 + 2048 * s: 6144 + 2048 * s]
    wblk4 = lambda s: scrB[:, 4096 + 2048 * (s % 4): 6144 + 2048 * (s % 4)]
    wvlt = lambda s: scrB[:, 8192 + 1024 * s: 9216 + 1024 * s]
    kslot = lambda s: scrB[:, 10240 + 1024 * s: 11264 + 1024 * s]
    vslot = lambda s: scrB[:, 12288 + 1024 * s: 13312 + 1024 * s]
    pt_off = lambda s: 1024 * s
    kq2v = scrB[:, 2048: 6144]
    ho_off = lambda p: 6144 + 4096 * p
    wot = lambda s: scrB[:, 14336 + 1024 * s: 14336 + 1024 * (s + 1)]

    x_c = lambda dc: x_res[:, T * dc: T * (dc + 1)]
    x_ch = lambda dc, th: x_res[:, T * dc + 512 * th: T * dc + 512 * (th + 1)]
    actc = lambda ic, th: act_bf[:, T * ic + 512 * th: T * ic + 512 * th + 512]

    def f(sy):
        sy.dma_start(out=tbl_sb[:, :], in_=din["tbl"].ap()).then_inc(sQ, 16)
        for name, dst, n in [("bk", bk_sb, 16), ("bq", bq_sb, 16), ("bv", bv_sb, 16),
                             ("bo", bo_sb, 16), ("bkl", bkl_sb, 8), ("bql", bql_sb, 8)]:
            sy.dma_start(out=dst[:, 0:n], in_=APx(din[name], 0, [[1, 128], [128, n]]),
                         allow_slow_non_contiguous=True).then_inc(sQ, 16)
        for i in range(4):
            sy.dma_start(
                out=APx(x_res, 4 * T * i, [[NCD * T, 128], [T, 4], [1, T]]),
                in_=APx(din["xt"], 4 * T * 128 * i, [[T, 128], [128 * T, 4], [1, T]]),
            ).then_inc(sQ, 16)
    P.add("sync", f)
    bump(sQ, 16 * 11)
    x_done = [16 * 7 + 16 * (i + 1) for i in range(4)]

    def f(gp):
        gp.memset(ones_f[:, :], 1.0)
        gp.memset(ones_b2[:, :], 1.0)
        gp.memset(negr[:, :], -0.0625)
        gp.memset(ones_bf[:, :], 1.0)
        gp.memset(bias4[:, 0:1], -0.2)
        gp.memset(bias4[:, 1:2], 0.2)
        gp.memset(bias4[:, 2:3], 1e-5)
        gp.memset(bias4[:, 5:6], 0.0)
        gp.iota(tmpM[:, :], [[-1, 128]], base=0, channel_multiplier=1,
                allow_small_or_imprecise_dtypes=True).then_inc(sG, 1)
    P.add("gpsimd", f)
    g_init = bump(sG, 1)

    def f(gp):
        for i in range(4):
            gp.iota(bcB[:, 0:512], [[-1, 512]], base=128 * i, channel_multiplier=1,
                    allow_small_or_imprecise_dtypes=True).then_inc(sG, 1)
            gp.wait_ge(sV, 1 + i)
        gp.nop()
    P.add("gpsimd", f, wait=[(sG, g_init)])

    for i in range(4):
        def f(v, i=i):
            v.tensor_single_scalar(mk4[:, 512 * i:512 * (i + 1)], bcB[:, 0:512], 0.0,
                                   op=OP.is_le).then_inc(sV, 1)
        P.add("vector", f, wait=[(sG, g_init + 1 + i)])
        bump(sV, 1)
    bump(sG, 4)

    def emit_ln_stats(ready, mode):
        mm_cnt = []
        for dc in range(NCD):
            s = dc % 2
            w = list(ready(dc))
            if dc >= 2:
                w.append((sPE, mm_cnt[dc - 2]))
            def f(sc, dc=dc, s=s):
                sc.activation(xbf2(s)[:, :], x_c(dc), AF.Copy)
                sc.activation(xsq2(s)[:, :], x_c(dc), AF.Square).then_inc(sA, 1)
            P.add("scalar", f, wait=w)
            a_done = bump(sA, 1)

            def f(t, dc=dc, s=s):
                st, sp = dc == 0, dc == NCD - 1
                t.matmul(psC[0:1, 0:512], ones_bf[:, 0:1], xbf2(s)[:, 0:512], start=st, stop=sp)
                t.matmul(psC[0:1, 512:1024], ones_bf[:, 0:1], xbf2(s)[:, 512:1024], start=st, stop=sp)
                t.matmul(psD[0:1, 0:512], ones_bf[:, 0:1], xsq2(s)[:, 0:512], start=st, stop=sp)
                t.matmul(psD[0:1, 512:1024], ones_bf[:, 0:1], xsq2(s)[:, 512:1024],
                         start=st, stop=sp).then_inc(sPE, 1)
            P.add("tensor", f, wait=[(sA, a_done)])
            mm_cnt.append(bump(sPE, 1))

        def f(v):
            v.tensor_scalar_mul(m_row[:, :], psC[0:1, :], 1.0 / D)
            v.tensor_scalar_mul(aux_row[:, :], psD[0:1, :], 1.0 / D)
            v.tensor_tensor(thrp_row[:, :], m_row[:, :], m_row[:, :], op=OP.mult)
            v.tensor_tensor(aux_row[:, :], aux_row[:, :], thrp_row[:, :],
                            op=OP.subtract).then_inc(sV, 1)
        P.add("vector", f, wait=[(sPE, mm_cnt[-1])])
        v1 = bump(sV, 1)

        def f(sc):
            sc.activation(aux_row[:, :], aux_row[:, :], AF.Sqrt,
                          bias=bias4[0:1, 2:3]).then_inc(sA, 1)
        P.add("scalar", f, wait=[(sV, v1)])
        a1 = bump(sA, 1)

        if mode == "thr":
            def f(v):
                v.scalar_tensor_tensor(thrp_row[:, :], aux_row[:, :], 0.33, m_row[:, :],
                                       op0=OP.mult, op1=OP.add)
                v.scalar_tensor_tensor(thrm_row[:, :], aux_row[:, :], -0.33, m_row[:, :],
                                       op0=OP.mult, op1=OP.add).then_inc(sV, 1)
            rows = (thrp_row, thrm_row)
        else:
            def f(v):
                v.reciprocal(aux_row[:, :], aux_row[:, :]).then_inc(sV, 1)
            rows = (m_row, aux_row)
        P.add("vector", f, wait=[(sA, a1)])
        v2 = bump(sV, 1)

        h0 = scrB[0:1, 10240:11264]
        l0 = scrB[0:1, 11264:12288]
        h1 = scrB[0:1, 12288:13312]
        l1 = scrB[0:1, 13312:14336]

        def f(v, rows=rows):
            v.tensor_copy(h0, rows[0][0:1, :])
            v.tensor_tensor(l0, rows[0][0:1, :], h0, op=OP.subtract)
            v.tensor_copy(h1, rows[1][0:1, :])
            v.tensor_tensor(l1, rows[1][0:1, :], h1, op=OP.subtract).then_inc(sV, 1)
        P.add("vector", f, wait=[(sV, v2)])
        v3 = bump(sV, 1)

        def f(t):
            ob = ones_b2[0:1, 0:128]
            last = None
            for th in range(2):
                sl = slice(512 * th, 512 * (th + 1))
                t.matmul(psC[:, sl], ob, h0[0:1, sl], start=True, stop=False)
                t.matmul(psC[:, sl], ob, l0[0:1, sl], start=False, stop=True)
                t.matmul(psD[:, sl], ob, h1[0:1, sl], start=True, stop=False)
                last = t.matmul(psD[:, sl], ob, l1[0:1, sl], start=False, stop=True)
            last.then_inc(sPE, 1)
        P.add("tensor", f, wait=[(sV, v3)])
        return (sPE, bump(sPE, 1))

    def emit_quantize(bc_w):
        half = None
        for dc in range(NCD):
            for th in range(2):
                def f(v, dc=dc, th=th):
                    sl = slice(512 * th, 512 * (th + 1))
                    v.tensor_tensor(gq_tmp[:, :], x_ch(dc, th), psC[:, sl], op=OP.is_gt)
                    v.tensor_tensor(lq_tmp[:, :], x_ch(dc, th), psD[:, sl], op=OP.is_lt)
                    v.tensor_tensor(actc(dc, th), gq_tmp[:, :], lq_tmp[:, :],
                                    op=OP.subtract).then_inc(sV, 1)
                P.add("vector", f, wait=[bc_w])
                bump(sV, 1)
            if dc == 7:
                half = P.cnt[sV.name]
        return P.cnt[sV.name], half

    es0 = ExitStack()
    aux_stacks.append(es0)
    fwtst = [es0.enter_context(nc.sbuf_tensor(f"fwtst{i}", [128, 2048], F32)) for i in range(2)]
    frtst = [es0.enter_context(nc.sbuf_tensor(f"frtst{i}", [128, 2048], F32)) for i in range(2)]
    s1t = [es0.enter_context(nc.sbuf_tensor(f"s1t{i}", [128, 2048], BF16)) for i in range(2)]
    s2t = [es0.enter_context(nc.sbuf_tensor(f"s2t{i}", [128, 2048], BF16)) for i in range(2)]

    def finish(wait):
        def f(sy):
            for i in range(4):
                sy.dma_start(
                    out=APx(out_t, 4 * T * 128 * i, [[T, 128], [128 * T, 4], [1, T]]),
                    in_=APx(x_res, 4 * T * i, [[NCD * T, 128], [T, 4], [1, T]]),
                ).then_inc(sQ, 16)
        P.add("sync", f, wait=wait)
        od = bump(sQ, 64)
        def f(sy):
            pass
        P.add("sync", f, wait=[(sQ, od)])
        with nc.Block() as block:
            P.realize(block)
        for st in reversed(aux_stacks):
            try:
                st.close()
            except Exception:
                pass
        es.close()
        return nc

    if stage == 9:
        return finish([(sQ, x_done[-1])])

    bc1 = emit_ln_stats(lambda dc: [(sQ, x_done[dc // 4])], "thr")
    if stage == 10:
        return finish([bc1])
    qz1, qz_half = emit_quantize(bc1)
    if stage == 11:
        return finish([(sV, qz1)])

    prep_q, prep_g, prep_a, prep_v = [], [], [], []
    wr2_st = []
    ffn1_mm, ffn1_evac = [], []
    for og in range(NCD):
        s = og % 2
        wl = [] if og < 2 else [(sV, prep_v[og - 2])]
        def f(sy, og=og, s=s):
            sy.dma_start(out=fwtst[s][:, :],
                         in_=APx(din["fwt"], og * 128, [[D, 128], [128 * D, 16], [1, 128]])).then_inc(sQ, 16)
        P.add("sync", f, wait=wl)
        prep_q.append(bump(sQ, 16))
        def f(gp, og=og, s=s):
            gp.dma_start(out=frtst[s][:, :],
                         in_=APx(din["frt"], og * 128, [[D, 128], [128 * D, 16], [1, 128]])).then_inc(sG, 16)
        P.add("gpsimd", f, wait=wl)
        prep_g.append(bump(sG, 16))

        wl = [(sQ, prep_q[og]), (sG, prep_g[og])]
        if og >= 2:
            wl.append((sV, prep_v[og - 2]))
        def f(sc, s=s):
            sc.activation(s1t[s][:, :], fwtst[s][:, :], AF.Sign, bias=bias4[:, 0:1])
            sc.activation(s2t[s][:, :], fwtst[s][:, :], AF.Sign, bias=bias4[:, 1:2]).then_inc(sA, 1)
        P.add("scalar", f, wait=wl)
        prep_a.append(bump(sA, 1))

        wl = [(sA, prep_a[og])]
        if og >= 4:
            wl.append((sPE, ffn1_mm[og - 4]))
            wl.append((sQ, wr2_st[og - 4]))
        def f(v, og=og, s=s):
            v.tensor_tensor(s1t[s][:, :], s1t[s][:, :], s2t[s][:, :], op=OP.add)
            v.scalar_tensor_tensor(wblk4(og)[:, :], s1t[s][:, :], 0.3, frtst[s][:, :],
                                   op0=OP.mult, op1=OP.add).then_inc(sV, 1)
        P.add("vector", f, wait=wl)
        prep_v.append(bump(sV, 1))

        if og >= 2:
            def f(sy, og2=og - 2):
                sy.dma_start(out=APx(wr2_hbm, og2 * 128 * D, [[D, 128], [1, D]]),
                             in_=wblk4(og2)[:, :]).then_inc(sQ, 16)
            P.add("sync", f, wait=[(sV, prep_v[og - 2])])
            wr2_st.append(bump(sQ, 16))

        if stage == 12:
            ffn1_mm.append(P.cnt.get(sPE.name, 0))
            ffn1_evac.append(P.cnt.get(sV.name, 0))
            continue
        if og >= 2:
            emit_ffn1_evac(og - 2)
        for half in range(2):
            wl = [(sV, prep_v[og])]
            wl.append((sV, qz_half if (half == 0 and og == 0) else qz1))
            if half == 1 and og == 0:
                wl = [(sV, prep_v[og]), (sV, qz1)]
            if half == 0 and og >= 2:
                wl.append((sV, ffn1_evac[og - 2]))
            def f(t, og=og, half=half):
                last = None
                for dc in range(8 * half, 8 * half + 8):
                    st, sp = dc == 0, dc == NCD - 1
                    lt = wblk4(og)[:, 128 * dc: 128 * (dc + 1)]
                    t.matmul(psS[og % 2][:, 0:512], lt, actc(dc, 0), start=st, stop=sp)
                    last = t.matmul(psS[og % 2][:, 512:1024], lt, actc(dc, 1), start=st, stop=sp)
                last.then_inc(sPE, 1)
            P.add("tensor", f, wait=wl)
            bump(sPE, 1)
        ffn1_mm.append(P.cnt[sPE.name])

        def emit_ffn1_evac(og2):
            def f(v, og2=og2):
                ps = psS[og2 % 2]
                v.scalar_tensor_tensor(x_ch(og2, 0), ps[:, 0:512], 0.25, x_ch(og2, 0),
                                       op0=OP.mult, op1=OP.add)
                v.scalar_tensor_tensor(x_ch(og2, 1), ps[:, 512:1024], 0.25, x_ch(og2, 1),
                                       op0=OP.mult, op1=OP.add).then_inc(sV, 1)
            P.add("vector", f, wait=[(sPE, ffn1_mm[og2])])
            ffn1_evac.append(bump(sV, 1))

    if stage != 12:
        emit_ffn1_evac(NCD - 2)
        emit_ffn1_evac(NCD - 1)
    for og2 in (NCD - 2, NCD - 1):
        def f(sy, og2=og2):
            sy.dma_start(out=APx(wr2_hbm, og2 * 128 * D, [[D, 128], [1, D]]),
                         in_=wblk4(og2)[:, :]).then_inc(sQ, 16)
        P.add("sync", f, wait=[(sV, prep_v[og2])])
        wr2_st.append(bump(sQ, 16))
    es0.close()
    aux_stacks.remove(es0)

    if stage == 12:
        return finish([(sV, prep_v[-1]), (sQ, P.cnt[sQ.name])])
    if stage == 0:
        return finish([(sV, ffn1_evac[-1])])

    es1 = ExitStack()
    aux_stacks.append(es1)
    c1 = es1.enter_context(nc.sbuf_tensor("c1", [128, NCD * T], BF16))
    bvl_bc = es1.enter_context(nc.sbuf_tensor("bvl_bc", [128, 1024], F32))
    tln = es1.enter_context(nc.sbuf_tensor("tln", [128, 512], F32))
    c1c = lambda ic, th: c1[:, T * ic + 512 * th: T * ic + 512 * th + 512]

    bc2 = emit_ln_stats(lambda dc: [(sV, ffn1_evac[dc])], "inv")
    xh_done = []
    for dc in range(NCD):
        for th in range(2):
            def f(v, dc=dc, th=th):
                sl = slice(512 * th, 512 * (th + 1))
                v.tensor_tensor(tln[:, :], x_ch(dc, th), psC[:, sl], op=OP.subtract)
                v.tensor_tensor(actc(dc, th), tln[:, :], psD[:, sl], op=OP.mult).then_inc(sV, 1)
            P.add("vector", f, wait=[bc2])
            xh_done.append(bump(sV, 1))
    xh_all = xh_done[-1]

    def f(sy):
        sy.dma_start(out=thrm_row[:, :], in_=APx(din["bvl"], 0, [[1024, 1], [1, 1024]])).then_inc(sQ, 16)
    P.add("sync", f, wait=[bc1])
    bvl_q = bump(sQ, 16)

    def wk_ap(name, oc):
        return APx(din[name], oc * 128, [[D, 128], [128 * D, 16], [1, 128]])

    def wl_ap(name, dlc):
        return APx(din[name], dlc * 128, [[DL, 128], [128 * DL, 16], [1, 128]])

    def emit_proj(n_in, n_out, rhs_col, wt_ap_fn, evac_fn, gate0):
        ld, mm, ev = [], [], []
        for oc in range(n_out):
            s = oc % 2
            wl = list(gate0) if oc < 2 else [(sPE, mm[oc - 2])]
            def f(gp, oc=oc, s=s):
                gp.dma_start(out=wblk(s)[:, 0:128 * n_in], in_=wt_ap_fn(oc)).then_inc(sG, 16)
            P.add("gpsimd", f, wait=wl)
            ld.append(bump(sG, 16))

            wl = [(sG, ld[oc])]
            if oc >= 2:
                wl.append(ev[oc - 2])
            def f(t, oc=oc, s=s):
                ps = psS[oc % 2]
                last = None
                for ic in range(n_in):
                    st, sp = ic == 0, ic == n_in - 1
                    lt = wblk(s)[:, 128 * ic: 128 * (ic + 1)]
                    t.matmul(ps[:, 0:512], lt, rhs_col(ic, 0), start=st, stop=sp)
                    last = t.matmul(ps[:, 512:1024], lt, rhs_col(ic, 1), start=st, stop=sp)
                last.then_inc(sPE, 1)
            P.add("tensor", f, wait=wl)
            mm.append(bump(sPE, 1))
            ev.append(evac_fn(oc, mm[oc]))
        return mm, ev

    def evac_c1(bias_sb):
        def ev(oc, mmc):
            def f(sc, oc=oc):
                ps = psS[oc % 2]
                sc.activation(c1c(oc, 0), ps[:, 0:512], AF.Identity, bias=bias_sb[:, oc:oc + 1])
                sc.activation(c1c(oc, 1), ps[:, 512:1024], AF.Identity,
                              bias=bias_sb[:, oc:oc + 1]).then_inc(sA, 1)
            P.add("scalar", f, wait=[(sPE, mmc)])
            return (sA, bump(sA, 1))
        return ev

    kq_dma = []

    def evac_kq(bias_sb, sect):
        def ev(dlc, mmc):
            s = dlc % 2
            wl = [(sPE, mmc)]
            if len(kq_dma) >= 2:
                wl.append((sQ, kq_dma[-2]))
            def f(sc, dlc=dlc, s=s):
                ps = psS[dlc % 2]
                sc.activation(kslot(s)[:, 0:512], ps[:, 0:512], AF.Identity, bias=bias_sb[:, dlc:dlc + 1])
                sc.activation(kslot(s)[:, 512:1024], ps[:, 512:1024], AF.Identity,
                              bias=bias_sb[:, dlc:dlc + 1]).then_inc(sA, 1)
            P.add("scalar", f, wait=wl)
            a = bump(sA, 1)
            def f(sy, dlc=dlc, s=s, sect=sect):
                sy.dma_start(out=APx(ccain, (sect * 8 + dlc) * CH, [[1024, 128], [1, 1024]]),
                             in_=kslot(s)[:, :]).then_inc(sQ, 16)
            P.add("sync", f, wait=[(sA, a)])
            kq_dma.append(bump(sQ, 16))
            return (sA, a)
        return ev

    g_ffn1 = [(sPE, ffn1_mm[-1]), (sV, xh_all)]
    k1_mm, k1_ev = emit_proj(NCD, NCD, actc, lambda oc: wk_ap("wk", oc),
                             evac_c1(bk_sb), g_ffn1)
    k2_mm, k2_ev = emit_proj(NCD, NDL, c1c, lambda oc: wl_ap("wkl", oc),
                             evac_kq(bkl_sb, 0), [k1_ev[-1]])
    q1_mm, q1_ev = emit_proj(NCD, NCD, actc, lambda oc: wk_ap("wq", oc),
                             evac_c1(bq_sb), [k2_ev[-1]])
    q2_mm, q2_ev = emit_proj(NCD, NDL, c1c, lambda oc: wl_ap("wql", oc),
                             evac_kq(bql_sb, 1), [q1_ev[-1]])
    kq_dma_all = kq_dma[-1]

    v1_mm, v1_ev = emit_proj(NCD, NCD, actc, lambda oc: wk_ap("wv", oc),
                             evac_c1(bv_sb), [q2_ev[-1]])

    def f(v):
        v.tensor_copy(scrB[0:1, 0:1024], thrm_row[0:1, :])
        v.tensor_tensor(scrB[0:1, 1024:2048], thrm_row[0:1, :],
                        scrB[0:1, 0:1024], op=OP.subtract).then_inc(sV, 1)
    P.add("vector", f, wait=[(sQ, bvl_q), (sV, xh_all)])
    bvl_v = bump(sV, 1)

    def f(t):
        ob = ones_b2[0:1, 0:128]
        last = None
        for th in range(2):
            sl = slice(512 * th, 512 * (th + 1))
            t.matmul(psD[:, sl], ob, scrB[0:1, 512 * th:512 * (th + 1)],
                     start=True, stop=False)
            last = t.matmul(psD[:, sl], ob, scrB[0:1, 1024 + 512 * th:1024 + 512 * (th + 1)],
                     start=False, stop=True)
        last.then_inc(sPE, 1)
    P.add("tensor", f, wait=[(sV, bvl_v)])
    bvl_pe = bump(sPE, 1)

    def f(sc):
        sc.activation(bvl_bc[:, :], psD[:, :], AF.Copy).then_inc(sA, 1)
    P.add("scalar", f, wait=[(sPE, bvl_pe)])
    bvl_a = bump(sA, 1)

    wvl_ld, vmm, v_ev, vdma = [], [], [], []
    for tcg in range(2):
        for oc in range(NCD):
            s = oc % 2
            if tcg == 0 and oc < 2:
                wl = [v1_ev[-1], (sA, bvl_a)]
            elif len(vmm) >= 2:
                wl = [(sPE, vmm[-1])]
            else:
                wl = []
            def f(gp, oc=oc, s=s):
                gp.dma_start(out=wvlt(s)[:, :],
                             in_=APx(din["wvl"], oc * 128 * DL, [[DL, 128], [1, DL]])).then_inc(sG, 16)
            P.add("gpsimd", f, wait=wl)
            wvl_ld.append(bump(sG, 16))

            wl = [(sG, wvl_ld[-1])]
            if tcg == 1 and oc == 0:
                wl.append(v_ev[-1])
            def f(t, tcg=tcg, oc=oc, s=s):
                st, sp = oc == 0, oc == NCD - 1
                last = None
                for j in range(4):
                    tc = 4 * tcg + j
                    lt = c1[:, T * oc + 128 * tc: T * oc + 128 * (tc + 1)]
                    t.matmul(psALL[j][:, 0:512], lt, wvlt(s)[:, 0:512], start=st, stop=sp)
                    last = t.matmul(psALL[j][:, 512:1024], lt, wvlt(s)[:, 512:1024], start=st, stop=sp)
                last.then_inc(sPE, 1)
            P.add("tensor", f, wait=wl)
            vmm.append(bump(sPE, 1))

        for j in range(4):
            tc = 4 * tcg + j
            s = tc % 2
            wl = [(sPE, vmm[-1])]
            if len(vdma) >= 2:
                wl.append((sQ, vdma[-2]))
            def f(v, j=j, s=s):
                v.tensor_tensor(vslot(s)[:, 0:512], psALL[j][:, 0:512], bvl_bc[:, 0:512], op=OP.add)
                v.tensor_tensor(vslot(s)[:, 512:1024], psALL[j][:, 512:1024],
                                bvl_bc[:, 512:1024], op=OP.add).then_inc(sV, 1)
            P.add("vector", f, wait=wl)
            v_ev.append((sV, bump(sV, 1)))
            def f(sy, tc=tc, s=s):
                sy.dma_start(out=APx(ccain, (16 + tc) * CH, [[1024, 128], [1, 1024]]),
                             in_=vslot(s)[:, :]).then_inc(sQ, 16)
            P.add("sync", f, wait=[v_ev[-1]])
            vdma.append(bump(sQ, 16))
    v_ev_last = v_ev[-1][1]

    def f(gp):
        gp.collective_compute("AllGather", OP.bypass, replica_groups=GROUPS,
                              ins=[ccain.ap().opt()],
                              outs=[ccaout.ap().opt()]).then_inc(sCC, 1)
    P.add("gpsimd", f, wait=[(sQ, vdma[-1])])
    cc_v = bump(sCC, 1)

    proj_all_mm = P.cnt[sPE.name]

    if stage == 13:
        tgt13 = P.cnt[sG.name] + 16 * 7
        def f(gp, tgt13=tgt13):
            gp.dma_start(out=dbg["d_xh"].ap(), in_=act_bf[:, :]).then_inc(sG, 16)
            gp.dma_start(out=dbg["d_c1"].ap(), in_=c1[:, :]).then_inc(sG, 16)
            gp.dma_start(out=dbg["d_bvl"].ap(), in_=bvl_bc[:, :]).then_inc(sG, 16)
            for i in range(4):
                gp.dma_start(out=APx(dbg["d_ccv2"], i * 2 * CH, [[1, 1], [1, 2 * CH]]),
                             in_=APx(ccain, 16 * CH + i * 2 * CH, [[1, 1], [1, 2 * CH]])).then_inc(sG, 16)
            gp.wait_ge(sG, tgt13)
        P.add("gpsimd", f, wait=[(sCC, cc_v), (sPE, proj_all_mm)])
        bump(sG, 16 * 7)
        def f(gp):
            gp.nop().then_inc(sG, 1)
        P.add("gpsimd", f)
        gv13 = bump(sG, 1)
        return finish([(sG, gv13)])

    if stage == 1:
        def f(gp):
            gp.nop().then_inc(sG, 1)
        P.add("gpsimd", f, wait=[(sCC, cc_v)])
        gv = bump(sG, 1)
        return finish([(sV, xh_all), (sG, gv)])

    es1.close()
    aux_stacks.remove(es1)
    es3 = ExitStack()
    aux_stacks.append(es3)
    k_att = [es3.enter_context(nc.sbuf_tensor(f"k_att{p}", [128, TL], BF16)) for p in range(2)]
    q_att = [es3.enter_context(nc.sbuf_tensor(f"q_att{p}", [128, TL], BF16)) for p in range(2)]
    v_aug = [es3.enter_context(nc.sbuf_tensor(f"v_aug{p}", [128, 32 * 130], BF16)) for p in range(2)]
    rr_bf = es3.enter_context(nc.sbuf_tensor("rr_bf", [33, 512], BF16))

    att_ld = []
    for p in range(2):
        wl = [(sCC, cc_v)]
        if p == 0:
            wl.append((sPE, proj_all_mm))
        def f(gp, p=p):
            gp.memset(v_aug[p][:, :], 1.0)
            gp.reg_load(rg0, tbl_sb[0:1, p:p + 1])
            gp.dma_start(out=APx(k_att[p], 0, [[TL, 128], [1024, 4], [1, 1024]]),
                         in_=bass.AP(ccaout, rg0, [[1024, 128], [B_A, 4], [1, 1024]])).then_inc(sG, 16)
            gp.reg_add(rg1, rg0, 8 * CH)
            gp.dma_start(out=APx(q_att[p], 0, [[TL, 128], [1024, 4], [1, 1024]]),
                         in_=bass.AP(ccaout, rg1, [[1024, 128], [B_A, 4], [1, 1024]])).then_inc(sG, 16)
            gp.reg_load(rg0, tbl_sb[0:1, 2 + p:3 + p])
            for r in range(4):
                for h in range(2):
                    gp.reg_add(rg1, rg0, r * B_A + 16 * CH + 64 * h)
                    gp.dma_start(out=APx(v_aug[p], r * 8 * 130 + 65 * h,
                                         [[32 * 130, 128], [130, 8], [1, 64]]),
                                 in_=bass.AP(ccaout, rg1,
                                             [[1024, 128], [CH, 8], [1, 64]])).then_inc(sG, 16)
        P.add("gpsimd", f, wait=wl)
        att_ld.append(bump(sG, 16 * 10))

    def wo_ap(oc):
        return APx(din["wo"], oc * 128, [[D, 128], [128 * D, 8], [1, 128]])
    wo_ld = []
    for oc in range(2):
        def f(gp, oc=oc):
            gp.dma_start(out=wot(oc % 2)[:, :], in_=wo_ap(oc)).then_inc(sG, 16)
        P.add("gpsimd", f, wait=[(sCC, cc_v)])
        wo_ld.append(bump(sG, 16))

    ho_mult_all = []
    rc_v_all = []
    for p in range(2):
        wl = [(sG, att_ld[p])]
        if p == 1:
            wl.append((sV, ho_mult_all[-1]))
        def f(sc, p=p):
            sc.activation(kq2v[:, :], k_att[p][:, :], AF.Square).then_inc(sA, 1)
        P.add("scalar", f, wait=wl)
        a_sq = bump(sA, 1)

        nrm_pe, nrm_v = [], []
        for which in range(2):
            if which == 1:
                def f(sc, p=p):
                    sc.activation(kq2v[:, :], q_att[p][:, :], AF.Square).then_inc(sA, 1)
                P.add("scalar", f, wait=[(sV, nrm_v[-1])])
                a_sq = bump(sA, 1)
            for c in range(8):
                bs = 512 * (c % 2)
                wl = [(sA, a_sq)]
                if c >= 2:
                    wl.append((sV, nrm_v[-2]))
                if p == 0 and which == 0 and c < 2:
                    wl.append((sV, v_ev_last))
                if p == 1 and which == 0 and c < 2:
                    wl.append((sV, ho_mult_all[-1]))
                def f(t, c=c, bs=bs):
                    t.matmul(psD[0:1, bs:bs + 512], ones_bf[:, 0:1],
                             kq2v[:, 512 * c: 512 * (c + 1)], start=True, stop=True).then_inc(sPE, 1)
                P.add("tensor", f, wait=wl)
                nrm_pe.append(bump(sPE, 1))
                def f(v, c=c, bs=bs, which=which):
                    dst = sc_row[0:1, which:which + 1]
                    if c == 0:
                        v.reduce_max(dst, psD[0:1, bs:bs + 512], axis=AXX).then_inc(sV, 1)
                    else:
                        v.reduce_max(sc_row[0:1, 2:3], psD[0:1, bs:bs + 512], axis=AXX)
                        v.tensor_tensor(dst, dst, sc_row[0:1, 2:3], op=OP.max).then_inc(sV, 1)
                P.add("vector", f, wait=[(sPE, nrm_pe[-1])])
                nrm_v.append(bump(sV, 1))

        def f(v):
            v.tensor_tensor(gq_tmp[0:1, 0:1], sc_row[0:1, 0:1], sc_row[0:1, 1:2], op=OP.add)
            v.tensor_copy(sc_row[0:1, 3:4], gq_tmp[0:1, 0:1])
            v.tensor_copy(sc_row[0:1, 4:5], sc_row[0:1, 3:4]).then_inc(sV, 1)
        P.add("vector", f, wait=[(sV, nrm_v[-1])])
        nb_v = bump(sV, 1)
        def f(t):
            t.matmul(psD[:, 0:1], negr[0:1, 0:128], gq_tmp[0:1, 0:1],
                     start=True, stop=True).then_inc(sPE, 1)
        P.add("tensor", f, wait=[(sV, nb_v)])
        nb_pe = bump(sPE, 1)
        def f(sc, p=p):
            sc.activation(bias4[:, 3 + p:4 + p], psD[:, 0:1], AF.Copy).then_inc(sA, 1)
        P.add("scalar", f, wait=[(sPE, nb_pe)])
        nb_a = bump(sA, 1)

        groups = [(qt, kb, grp) for qt in range(8) for kb in range(qt + 1) for grp in range(2)]
        G = len(groups)
        PT = lambda s: 2048 * s
        s_inc, exp_done, mask_v, av_inc = {}, {}, {}, {}
        qt_last_g = {qt: max(g for g, (q_, _, _) in enumerate(groups) if q_ == qt)
                     for qt in range(8)}
        pend_bcast = None

        def emit_S(g):
            qt, kb, grp = groups[g]
            wl = []
            if g == 0:
                wl.append((sA, nb_a))
                if p == 0:
                    wl.append((sV, v_ev_last))
            else:
                wl.append((sA, exp_done[g - 1]))
            def f(t, p=p, qt=qt, kb=kb, grp=grp):
                last = None
                for kloc in range(2):
                    gblk = 4 * kb + 2 * grp + kloc
                    for h in range(2):
                        last = t.matmul(
                            psAB[:, 512 * (2 * kloc + h): 512 * (2 * kloc + h) + 512],
                            k_att[p][64 * h: 64 * h + 64, 128 * gblk: 128 * (gblk + 1)],
                            q_att[p][64 * h: 64 * h + 64, 512 * qt: 512 * (qt + 1)],
                            start=True, stop=True, tile_position=(64 * h, 0))
                last.then_inc(sPE, 1)
            P.add("tensor", f, wait=wl)
            s_inc[g] = bump(sPE, 1)

        def emit_exp(g):
            s = g % 2
            def f(sc, p=p, s=s):
                sc.activation(scrB[:, PT(s): PT(s) + 2048], psAB[:, 0:2048],
                              AF.Exp, bias=bias4[:, 3 + p:4 + p],
                              scale=0.125).then_inc(sA, 1)
            P.add("scalar", f, wait=[(sPE, s_inc[g])])
            exp_done[g] = bump(sA, 1)

        def emit_masks(g):
            qt, kb, grp = groups[g]
            s = g % 2
            def f(v, s=s, grp=grp):
                for kloc in range(2):
                    i = 2 * grp + kloc
                    for h in range(2):
                        c0 = PT(s) + 512 * (2 * kloc + h)
                        v.tensor_tensor(scrB[:, c0:c0 + 512], scrB[:, c0:c0 + 512],
                                        mk4[:, 512 * i:512 * (i + 1)], op=OP.mult)
                v.tensor_copy(lq_tmp[0:1, 0:8],
                              scrB[0:1, PT(s):PT(s) + 8]).then_inc(sV, 1)
            P.add("vector", f, wait=[(sA, exp_done[g])])
            mask_v[g] = bump(sV, 1)

        def emit_AV(g):
            qt, kb, grp = groups[g]
            s = g % 2
            st = (kb == 0 and grp == 0)
            sp = (kb == qt and grp == 1)
            avoff = 512 * (qt % 2)
            wl = [(sA, exp_done[g])]
            if g in mask_v:
                wl.append((sV, mask_v[g]))
            if st:
                if len(ho_mult_all) >= 2:
                    wl.append((sV, ho_mult_all[-2]))
                if rc_v_all:
                    wl.append((sV, rc_v_all[-1]))
            def f(t, p=p, kb=kb, grp=grp, s=s, st=st, sp=sp, avoff=avoff):
                for kloc in range(2):
                    gblk = 4 * kb + 2 * grp + kloc
                    pt0 = scrB[:, PT(s) + 512 * (2 * kloc): PT(s) + 512 * (2 * kloc) + 512]
                    pt1 = scrB[:, PT(s) + 512 * (2 * kloc + 1): PT(s) + 512 * (2 * kloc + 1) + 512]
                    t.matmul(psC[0:64, avoff:avoff + 512],
                             v_aug[p][:, 130 * gblk: 130 * gblk + 64],
                             pt0, start=st and kloc == 0, stop=sp and kloc == 1,
                             tile_position=(0, 0))
                    t.matmul(psC[64:128, avoff:avoff + 512],
                             v_aug[p][:, 130 * gblk + 65: 130 * gblk + 129],
                             pt1, start=st and kloc == 0, stop=sp and kloc == 1,
                             tile_position=(0, 64))
                    t.matmul(psD[0:1, 0:512],
                             v_aug[p][:, 130 * gblk + 64: 130 * gblk + 65],
                             pt0, start=st and kloc == 0, stop=sp and kloc == 1,
                             tile_position=(0, 0))
                    last = t.matmul(psD[32:33, 0:512],
                                    v_aug[p][:, 130 * gblk + 129: 130 * gblk + 130],
                                    pt1, start=st and kloc == 0, stop=sp and kloc == 1,
                                    tile_position=(0, 32))
                last.then_inc(sPE, 1)
            P.add("tensor", f, wait=wl)
            av_inc[g] = bump(sPE, 1)

        def emit_recip(qt):
            def f(v):
                with nc.allow_low_precision(reason="bf16 softmax denominator recip"):
                    v.reciprocal(rr_bf[0:1, 0:512], psD[0:1, 0:512])
                    v.reciprocal(rr_bf[32:33, 0:512], psD[32:33, 0:512]).then_inc(sV, 1)
            P.add("vector", f, wait=[(sPE, av_inc[qt_last_g[qt]])])
            rc_v_all.append(bump(sV, 1))

        def emit_bcast_and_evac(qt):
            def f(t):
                t.matmul(psD[0:64, 512:1024], ones_b2[0:1, 0:64], rr_bf[0:1, 0:512],
                         start=True, stop=True, tile_position=(0, 0))
                t.matmul(psD[64:128, 512:1024], ones_b2[32:33, 0:64], rr_bf[32:33, 0:512],
                         start=True, stop=True, tile_position=(32, 64)).then_inc(sPE, 1)
            P.add("tensor", f, wait=[(sV, rc_v_all[-1])])
            bc_pe = bump(sPE, 1)
            def f(v, p=p, qt=qt):
                v.tensor_copy(bcB[:, 0:512], psD[:, 512:1024])
                hob = ho_off(p) + 512 * qt
                avoff = 512 * (qt % 2)
                v.tensor_tensor(scrB[0:64, hob:hob + 512],
                                psC[0:64, avoff:avoff + 512], bcB[0:64, 0:512], op=OP.mult)
                v.tensor_tensor(scrB[64:128, hob:hob + 512],
                                psC[64:128, avoff:avoff + 512], bcB[64:128, 0:512],
                                op=OP.mult).then_inc(sV, 1)
            P.add("vector", f, wait=[(sPE, bc_pe)])
            ho_mult_all.append(bump(sV, 1))

        for g in range(G + 1):
            if g < G:
                emit_S(g)
            if pend_bcast is not None and g >= pend_bcast[1] + 1:
                emit_bcast_and_evac(pend_bcast[0])
                pend_bcast = None
            if g >= 1:
                emit_AV(g - 1)
                qtp, kbp, grpp = groups[g - 1]
                if g - 1 == qt_last_g[qtp]:
                    emit_recip(qtp)
                    pend_bcast = (qtp, g)
            if g < G:
                emit_exp(g)
                if groups[g][1] == groups[g][0]:
                    emit_masks(g)
        if pend_bcast is not None:
            emit_bcast_and_evac(pend_bcast[0])
            pend_bcast = None

        def f(sy, p=p):
            sy.dma_start(out=APx(cc2_in, p * 128 * TL, [[TL, 128], [1, TL]]),
                         in_=APx(scrB, ho_off(p), [[16384, 128], [1, TL]])).then_inc(sQ, 16)
        P.add("sync", f, wait=[(sV, ho_mult_all[-1])])
        bump(sQ, 16)
    cc2_wr = P.cnt[sQ.name]
    att_pe_all = P.cnt[sPE.name]

    if stage == 20:
        tgt20 = P.cnt[sG.name] + 128
        def f(gp, tgt20=tgt20):
            gp.dma_start(out=dbg["d_k0"].ap(), in_=k_att[0][:, :]).then_inc(sG, 16)
            gp.dma_start(out=dbg["d_q0"].ap(), in_=q_att[0][:, :]).then_inc(sG, 16)
            gp.dma_start(out=dbg["d_va0"].ap(), in_=v_aug[0][:, :]).then_inc(sG, 16)
            gp.dma_start(out=dbg["d_ho"].ap(), in_=scrB[:, 6144:14336]).then_inc(sG, 16)
            gp.dma_start(out=dbg["d_bias"].ap(), in_=bias4[:, :]).then_inc(sG, 16)
            gp.dma_start(out=APx(dbg["d_ccv"], 0, [[1, 1], [1, B_V]]),
                         in_=APx(ccain, 16 * CH, [[1, 1], [1, B_V]])).then_inc(sG, 16)
            gp.dma_start(out=APx(dbg["d_scrow"], 0, [[16, 1], [1, 16]]), in_=sc_row[:, :]).then_inc(sG, 16)
            gp.dma_start(out=dbg["d_kq2"].ap(), in_=kq2v).then_inc(sG, 16)
            gp.wait_ge(sG, tgt20)
        P.add("gpsimd", f, wait=[(sQ, cc2_wr)])
        bump(sG, 128)

    def f(gp):
        gp.collective_compute("AllGather", OP.bypass, replica_groups=GROUPS,
                              ins=[cc2_in.ap().opt()],
                              outs=[cc2_out.ap().opt()]).then_inc(sCC, 1)
    P.add("gpsimd", f, wait=[(sQ, cc2_wr)])
    cc_2 = bump(sCC, 1)

    es3.close()
    aux_stacks.remove(es3)
    es4 = ExitStack()
    aux_stacks.append(es4)
    ho_all = es4.enter_context(nc.sbuf_tensor("ho_all", [128, NDL * 1024], BF16))
    hoc = lambda ic, th: ho_all[:, 1024 * ic + 512 * th: 1024 * ic + 512 * th + 512]

    def f(gp):
        gp.reg_load(rg0, tbl_sb[0:1, 4:5])
        gp.dma_start(out=APx(ho_all, 0, [[NDL * 1024, 128], [1024, 8], [1, 1024]]),
                     in_=bass.AP(cc2_out, rg0, [[TL, 128], [128 * TL, 8], [1, 1024]])).then_inc(sG, 16)
    P.add("gpsimd", f, wait=[(sCC, cc_2)])
    ho_ld = bump(sG, 16)

    wo_mm, wo_ev = [], []
    for oc in range(NCD):
        s = oc % 2
        if oc >= 2:
            def f(gp, oc=oc, s=s):
                gp.dma_start(out=wot(s)[:, :], in_=wo_ap(oc)).then_inc(sG, 16)
            P.add("gpsimd", f, wait=[(sPE, wo_mm[oc - 2])])
            wo_ld.append(bump(sG, 16))

        wl = [(sG, wo_ld[oc]), (sG, ho_ld)]
        if oc >= 2:
            wl.append((sV, wo_ev[oc - 2]))
        def f(t, s=s):
            ps = psS[s]
            last = None
            for ic in range(NDL):
                st, sp = ic == 0, ic == NDL - 1
                lt = wot(s)[:, 128 * ic: 128 * (ic + 1)]
                t.matmul(ps[:, 0:512], lt, hoc(ic, 0), start=st, stop=sp)
                last = t.matmul(ps[:, 512:1024], lt, hoc(ic, 1), start=st, stop=sp)
            last.then_inc(sPE, 1)
        P.add("tensor", f, wait=wl)
        wo_mm.append(bump(sPE, 1))

        def f(v, oc=oc, s=s):
            ps = psS[s]
            v.scalar_tensor_tensor(x_ch(oc, 0), ps[:, 0:512], bo_sb[:, oc:oc + 1],
                                   x_ch(oc, 0), op0=OP.add, op1=OP.add)
            v.scalar_tensor_tensor(x_ch(oc, 1), ps[:, 512:1024], bo_sb[:, oc:oc + 1],
                                   x_ch(oc, 1), op0=OP.add, op1=OP.add).then_inc(sV, 1)
        P.add("vector", f, wait=[(sPE, wo_mm[oc])])
        wo_ev.append(bump(sV, 1))

    if stage == 20:
        tgt21 = P.cnt[sG.name] + 16
        def f(gp, tgt21=tgt21):
            gp.dma_start(out=dbg["d_hoall"].ap(), in_=ho_all[:, :]).then_inc(sG, 16)
            gp.wait_ge(sG, tgt21)
        P.add("gpsimd", f, wait=[(sG, tgt21 - 16)])
        bump(sG, 16)
    if stage in (2, 20):
        return finish([(sV, wo_ev[-1])])

    bc3 = emit_ln_stats(lambda dc: [(sV, wo_ev[dc])], "thr")
    qz3, qz3_half = emit_quantize(bc3)

    ffn2_mm, ffn2_ev = [], []
    for og in range(NCD):
        s = og % 2
        wl = [(sG, ho_ld)] if og < 2 else [(sPE, ffn2_mm[og - 2])]
        def f(sy, og=og, s=s):
            sy.dma_start(out=wblk(s)[:, :],
                         in_=APx(wr2_hbm, og * 128 * D, [[D, 128], [1, D]])).then_inc(sQ, 16)
        P.add("sync", f, wait=wl)
        ld = bump(sQ, 16)

        for half in range(2):
            wl = [(sQ, ld)]
            wl.append((sV, qz3_half if (half == 0 and og == 0) else qz3))
            if half == 0 and og >= 2:
                wl.append((sV, ffn2_ev[og - 2]))
            def f(t, s=s, half=half):
                last = None
                for dc in range(8 * half, 8 * half + 8):
                    st, sp = dc == 0, dc == NCD - 1
                    lt = wblk(s)[:, 128 * dc: 128 * (dc + 1)]
                    t.matmul(psS[s][:, 0:512], lt, actc(dc, 0), start=st, stop=sp)
                    last = t.matmul(psS[s][:, 512:1024], lt, actc(dc, 1), start=st, stop=sp)
                last.then_inc(sPE, 1)
            P.add("tensor", f, wait=wl)
            bump(sPE, 1)
        ffn2_mm.append(P.cnt[sPE.name])

        def f(v, og=og, s=s):
            ps = psS[s]
            v.scalar_tensor_tensor(x_ch(og, 0), ps[:, 0:512], 0.25, x_ch(og, 0),
                                   op0=OP.mult, op1=OP.add)
            v.scalar_tensor_tensor(x_ch(og, 1), ps[:, 512:1024], 0.25, x_ch(og, 1),
                                   op0=OP.mult, op1=OP.add).then_inc(sV, 1)
        P.add("vector", f, wait=[(sPE, ffn2_mm[og])])
        ffn2_ev.append(bump(sV, 1))

    def f(sy):
        for i in range(4):
            sy.dma_start(
                out=APx(out_t, 4 * T * 128 * i, [[T, 128], [128 * T, 4], [1, T]]),
                in_=APx(x_res, 4 * T * i, [[NCD * T, 128], [T, 4], [1, T]]),
            ).then_inc(sQ, 16)
    P.add("sync", f, wait=[(sV, ffn2_ev[-1])])
    out_done = bump(sQ, 64)

    def f(sy):
        pass
    P.add("sync", f, wait=[(sQ, out_done)])

    with nc.Block() as block:
        P.realize(block)
    es4.close()
    es.close()
    return nc


_NC_CACHE = None


def kernel(__stage=99, **inputs):
    global _NC_CACHE
    x = np.asarray(inputs["L0_attn_x"], dtype=np.float32)
    W = {k: np.asarray(v, dtype=np.float32) for k, v in inputs.items() if k != "L0_attn_x"}

    if _NC_CACHE is None or _NC_CACHE[0] != __stage:
        _NC_CACHE = (__stage, build_nc(stage=__stage))
    nc = _NC_CACHE[1]

    shared = {
        "wk": np.ascontiguousarray(W["L0_attn_wk"]),
        "wq": np.ascontiguousarray(W["L0_attn_wq"]),
        "wv": np.ascontiguousarray(W["L0_attn_wv"]),
        "wkl": np.ascontiguousarray(W["L0_attn_wkl"]),
        "wql": np.ascontiguousarray(W["L0_attn_wql"]),
        "wvl": np.ascontiguousarray(W["L0_attn_wvl"]),
        "wo": np.ascontiguousarray(W["L0_attn_wo"]),
        "bk": np.ascontiguousarray(W["L0_attn_bk"]),
        "bq": np.ascontiguousarray(W["L0_attn_bq"]),
        "bv": np.ascontiguousarray(W["L0_attn_bv"]),
        "bo": np.ascontiguousarray(W["L0_attn_bo"]),
        "bkl": np.ascontiguousarray(W["L0_attn_bkl"]),
        "bql": np.ascontiguousarray(W["L0_attn_bql"]),
        "bvl": np.ascontiguousarray(W["L0_attn_bvl"]),
        "fwt": np.ascontiguousarray(W["L0_ffn_weight"].T),
        "frt": np.ascontiguousarray(W["L0_ffn_rram_variation"].T),
    }
    in_maps = []
    for c in range(8):
        b, qq = divmod(c, 4)
        m = dict(shared)
        m["xt"] = np.ascontiguousarray(x[b, T * qq: T * (qq + 1), :].T)
        hc = 2 * qq
        m["tbl"] = np.array([[hc * CH, (hc + 1) * CH, 128 * hc, 128 * (hc + 1),
                              qq * 1024, 0, 0, 0]], dtype=np.int32)
        in_maps.append(m)

    res = run_bass_kernel_spmd(nc, in_maps, core_ids=list(range(8)))
    out = np.empty((2, TL, D), dtype=np.float32)
    for c in range(8):
        b, qq = divmod(c, 4)
        out[b, T * qq: T * (qq + 1), :] = res.results[c]["out_t"].T
    return out
